# version 05
# baseline (speedup 1.0000x reference)
import numpy as np
import concourse.bass as bass
import concourse.mybir as mybir
from concourse.bass_utils import run_bass_kernel_spmd

F32 = mybir.dt.float32
BF16 = mybir.dt.bfloat16
I32 = mybir.dt.int32
AF = mybir.ActivationFunctionType
ALU = mybir.AluOpType

NSEQ = 4
T = 2048
D = 1024
S = 32
LC = 128
NCH = T // LC
TT = 512
NTT = T // TT
H3 = 3072
BN = float(1.0 / np.sqrt(1.0 + 1e-5))
YPW = NCH * 2 * LC

DBG_NTT = None
DBG_STAGE = 99
DBG_PASSB = True
DBG_SCONST = True


def dt_size(dt):
    return {F32: 4, BF16: 2, I32: 4}[dt]


class SBA:
    def __init__(self, nc, total_bytes):
        self.nc = nc
        self.total = total_bytes
        self.arena = nc.alloc_sbuf_tensor("arena", [128, total_bytes // 4], F32)
        self.off = 0
        self.marks = []
        self.peak = 0

    def alloc(self, parts, n, dtype):
        sz = (n * dt_size(dtype) + 31) & ~31
        assert self.off + sz <= self.total, f"SBUF overflow {self.off + sz}"
        a = self.arena.ap()[0:parts, self.off // 4:(self.off + sz) // 4]
        if dtype != F32:
            a = a.bitcast(dtype)
        self.off += sz
        self.peak = max(self.peak, self.off)
        return a[:, 0:n]

    def mark(self):
        self.marks.append(self.off)

    def reset(self):
        self.off = self.marks.pop()


class Buf:
    __slots__ = ("lw", "lr", "nowaw", "excl", "ds", "ds_cls")

    def __init__(self, nowaw=False, excl=False):
        self.lw = None
        self.lr = {}
        self.nowaw = nowaw
        self.excl = excl
        self.ds = None
        self.ds_cls = None


class Prog:
    ENG = ("sync", "tensor", "vector", "scalar", "gpsimd")

    def __init__(self, nc):
        self.nc = nc
        self.steps = {e: [] for e in self.ENG}
        self.semh = {e: nc.alloc_semaphore(name=f"S_{e}") for e in self.ENG}
        self.cnt = {e: 0 for e in self.ENG}
        self.waited = {e: {} for e in self.ENG}
        self.pending_fence = {}
        self.ndsem = 0
        self.free_sems = []
        self.sem_owner = {}

    def bufsem(self, buf, swdge):
        if buf.ds is None:
            if not swdge and self.free_sems:
                buf.ds = self.free_sems.pop()
            else:
                name = f"D{self.ndsem}"
                self.ndsem += 1
                buf.ds = self.nc.alloc_semaphore(name=name)
                self.semh[name] = buf.ds
                self.cnt.setdefault(name, 0)
            buf.ds_cls = swdge
            if not swdge:
                self.sem_owner[buf.ds.name] = buf
        assert buf.ds_cls == swdge, "SWDGE and HWDGE DMAs cannot share a semaphore"
        return buf.ds.name

    def fence(self):
        snap = [(k, v) for k, v in self.cnt.items()]
        for e in self.ENG:
            self.pending_fence[e] = snap
        for name, buf in self.sem_owner.items():
            buf.ds = None
            self.free_sems.append(self.semh[name])
        self.sem_owner = {}

    SELF_ORDERED = ()

    def _resolve(self, eng, waits):
        out = []
        for s, v in waits:
            if s == eng and eng in self.SELF_ORDERED:
                continue
            if v > self.waited[eng].get(s, 0):
                out.append((s, v))
                self.waited[eng][s] = v
        return out

    def raw(self, eng, fn, waits=(), inc=1, inc_key=None):
        waits = list(waits)
        if eng in self.pending_fence:
            waits = list(self.pending_fence.pop(eng)) + waits
        w = self._resolve(eng, waits)
        key = inc_key or eng
        if inc:
            self.cnt[key] = self.cnt.get(key, 0) + inc
        tgt = self.cnt.get(key, 0)
        semh = self.semh

        def step(engobj, w=w, fn=fn, inc=inc, key=key):
            for s, v in w:
                engobj.wait_ge(semh[s], v)
            ins = fn(engobj)
            if inc:
                ins.then_inc(semh[key], inc)
        self.steps[eng].append(step)
        return tgt

    @staticmethod
    def _hazard_waits(reads, writes, extra):
        waits = list(extra)
        for b in reads:
            if b.nowaw:
                waits.extend((b.lw or {}).items())
            elif b.lw:
                waits.append(b.lw)
            if b.excl:
                waits.extend(b.lr.items())
        for b in writes:
            if b.lw and not b.nowaw:
                waits.append(b.lw)
            waits.extend(b.lr.items())
        return waits

    @staticmethod
    def _hazard_record(reads, writes, key, c):
        for b in reads:
            b.lr[key] = c
        for b in writes:
            if b.nowaw:
                if b.lw is None:
                    b.lw = {}
                b.lw[key] = max(b.lw.get(key, 0), c)
            else:
                b.lw = (key, c)
            b.lr = {}

    def _dma_key(self, eng, reads, writes):
        assert len(writes) == 1
        wb = writes[0]
        swdge = eng == "gpsimd"
        if wb.nowaw:
            srcs = [b for b in reads if not b.nowaw]
            assert len(srcs) == 1, "nowaw DMA write needs one serialized source"
            return self.bufsem(srcs[0], swdge)
        return self.bufsem(wb, swdge)

    def op(self, eng, fn, reads=(), writes=(), extra=(), inc=1):
        waits = self._hazard_waits(reads, writes, extra)
        key = self._dma_key(eng, reads, writes) if inc >= 16 else None
        c = self.raw(eng, fn, waits=waits, inc=inc, inc_key=key)
        self._hazard_record(reads, writes, key or eng, c)
        return c

    def group(self, eng, fns, reads=(), writes=(), extra=(), inc_last=1):
        waits = self._hazard_waits(reads, writes, extra)
        isdma = inc_last >= 16
        key = None
        if isdma:
            assert inc_last == 16 * len(fns)
            key = self._dma_key(eng, reads, writes)
        c = None
        for i, fn in enumerate(fns):
            inc = 16 if isdma else (inc_last if i == len(fns) - 1 else 0)
            c = self.raw(eng, fn, waits=(waits if i == 0 else ()), inc=inc, inc_key=key)
        self._hazard_record(reads, writes, key or eng, c)
        return c

    def emit(self):
        nc = self.nc
        with nc.Block() as block:
            @block.sync
            def _(eng):
                for st in self.steps["sync"]:
                    st(eng)

            @block.tensor
            def _(eng):
                for st in self.steps["tensor"]:
                    st(eng)

            @block.vector
            def _(eng):
                for st in self.steps["vector"]:
                    st(eng)

            @block.scalar
            def _(eng):
                for st in self.steps["scalar"]:
                    st(eng)

            @block.gpsimd
            def _(eng):
                for st in self.steps["gpsimd"]:
                    st(eng)


def build_program(dbg=(), phases=("p1", "p2", "p3", "p4")):
    nc = bass.Bass("TRN2", target_bir_lowering=False, debug=False)

    def dram(name, shape, dtype, kind="ExternalInput"):
        if name in dbg and kind == "Internal":
            kind = "ExternalOutput"
        return nc.dram_tensor(name, list(shape), dtype, kind=kind).ap()

    x_in = dram("x", (NSEQ, T, D), F32)
    mw = {}
    for m in ("m1", "m2"):
        mw[m] = dict(
            in_w=dram(f"{m}_in_w", (D, 2 * D), F32),
            in_b=dram(f"{m}_in_b", (128, 16), F32),
            conv_w=dram(f"{m}_conv_w", (128, 32), F32),
            conv_b=dram(f"{m}_conv_b", (128, 8), F32),
            ssm_w=dram(f"{m}_ssm_w", (D, S), F32),
            ssm_wT=dram(f"{m}_ssm_wT", (S, D), F32),
            ssm_b=dram(f"{m}_ssm_b", (S, 1), F32),
            A=dram(f"{m}_A", (S, S), F32),
            Bv=dram(f"{m}_B", (S, 1), F32),
            Cv=dram(f"{m}_C", (S, 1), F32),
            Dv=dram(f"{m}_D", (128, 8), F32),
            out_w=dram(f"{m}_out_w", (D, D), F32),
            out_b=dram(f"{m}_out_b", (128, 8), F32),
        )
    c1wT = dram("c1wT", (D, 3 * 512), F32)
    c1b = dram("c1b", (128, 4), F32)
    c4wT = dram("c4wT", (D, 512), F32)
    c5wT = dram("c5wT", (D, 3 * D), F32)
    wgT = dram("wgT", (D, H3), I32)
    wuT = dram("wuT", (D, H3), I32)
    wdT = dram("wdT", (H3, D), I32)

    xnb_h = dram("xnb_h", (NSEQ, D, T), BF16, kind="Internal")
    xm1b_h = dram("xm1b_h", (NSEQ, D, T), BF16, kind="Internal")
    xm2b_h = dram("xm2b_h", (NSEQ, D, T), BF16, kind="Internal")
    s16_h = dram("s16_h", (NSEQ, D, T), BF16, kind="Internal")
    wg16 = dram("wg16", (D, H3), BF16, kind="Internal")
    c1w16 = dram("c1w16", (D, 3 * 512), BF16, kind="Internal")
    c4w16 = dram("c4w16", (D, 512), BF16, kind="Internal")
    c5w16 = dram("c5w16", (D, 3 * D), BF16, kind="Internal")
    wu16 = dram("wu16", (D, H3), BF16, kind="Internal")
    wd16 = dram("wd16", (H3, D), BF16, kind="Internal")
    w_h = dram("w_h", (NSEQ, D, T), BF16, kind="Internal")
    sc2 = dram("scan2", (2, S, (LC + 2) * S), F32, kind="Internal")
    c_h = dram("c_h", (NSEQ, T), BF16, kind="Internal")
    y_hbm = dram("y_hbm", (NSEQ, 128, YPW), BF16, kind="Internal")
    out_t = dram("out", (NSEQ, T, D), F32, kind="ExternalOutput")

    ident_h = nc.inline_tensor(np.eye(128, dtype=np.float32), name="ident128").ap()
    onesc_h = nc.inline_tensor(np.ones((128, 1), dtype=np.float32), name="ones128").ap()

    p = Prog(nc)
    sb = SBA(nc, 206 * 1024)

    PS = {}
    for nm, wdt in (("A", 512), ("B", 512), ("C", 512), ("D", 512), ("G", 1024),
                    ("H", 1024)):
        PS[nm] = nc.alloc_psum_tensor(f"pb{nm}", [128, wdt], F32).ap()
    PS["Ga"] = PS["G"][:, 0:512]
    PS["Gb"] = PS["G"][:, 512:1024]
    PS["Ha"] = PS["H"][:, 0:512]
    PS["Hb"] = PS["H"][:, 512:1024]
    pB = {nm: Buf(excl=True) for nm in PS}
    pB["G"] = None
    pB["H"] = None
    pBG = Buf(excl=True)
    pBH = Buf(excl=True)
    pB["G"] = pBG
    pB["H"] = pBH

    ident = sb.alloc(128, 128, F32)
    ident_b = sb.alloc(128, 128, BF16)
    b_ident = Buf()
    b_identb = Buf()
    p.op("sync", lambda e: e.dma_start(out=ident, in_=ident_h), writes=[b_ident], inc=16)
    p.op("vector", lambda e: e.tensor_copy(ident_b, ident), reads=[b_ident], writes=[b_identb])
    cst = sb.alloc(128, 4, F32)
    b_cst = Buf()
    onesc = sb.alloc(128, 1, F32)
    onesc_b = sb.alloc(128, 1, BF16)
    b_ones = Buf()
    b_onesb = Buf()
    p.op("sync", lambda e: e.dma_start(out=onesc, in_=onesc_h), writes=[b_ones], inc=16)
    p.op("vector", lambda e: e.tensor_copy(onesc_b, onesc), reads=[b_ones], writes=[b_onesb])
    p.op("vector", lambda e: e.memset(cst[:, 0:1], 1e-5), writes=[b_cst])
    p.op("vector", lambda e: e.memset(cst[:, 1:2], float(BN)), reads=[b_cst], writes=[b_cst])
    p.op("vector", lambda e: e.memset(cst[:, 2:3], float(1.0 / D)), reads=[b_cst], writes=[b_cst])
    p.op("vector", lambda e: e.memset(cst[:, 3:4], float(1.0 / 127.0)), reads=[b_cst], writes=[b_cst])

    sb.mark()
    PTw = sb.alloc(S, (LC + 2) * S, F32)
    Vmat = sb.alloc(S, LC + 1, F32)
    VmatB = sb.alloc(S, LC, F32)
    A_sb = sb.alloc(S, S, F32)
    Bv_sb = sb.alloc(S, 1, F32)
    Cv_sb = sb.alloc(S, 1, F32)
    PTb16 = sb.alloc(S, LC * S, BF16)
    bPT, bVm, bVmB, bA, bBv, bCv = (Buf() for _ in range(6))
    b_PTb = Buf()

    def build_scan_consts(m, PT, Vm, VmB, A_s, Bv_s, Cv_s, bPT, bVm, bVmB, bA, bBv, bCv, psum, bpsum):
        w = mw[m]
        p.op("sync", lambda e: e.dma_start(out=A_s, in_=w["A"]), writes=[bA], inc=16)
        p.op("sync", lambda e: e.dma_start(out=Bv_s, in_=w["Bv"]), writes=[bBv], inc=16)
        p.op("sync", lambda e: e.dma_start(out=Cv_s, in_=w["Cv"]), writes=[bCv], inc=16)
        p.op("vector", lambda e: e.tensor_copy(PT[:, 0:S], ident[0:S, 0:S]),
             reads=[b_ident], writes=[bPT])
        p.op("vector", lambda e: e.tensor_copy(Vm[:, 0:1], Cv_s), reads=[bCv], writes=[bVm])
        for k in range(1, LC + 2):
            p.op("tensor", lambda e, k=k: e.matmul(
                psum[0:S, 0:S], A_s, PT[:, (k - 1) * S:k * S], start=True, stop=True),
                reads=[bA, bPT], writes=[bpsum])
            p.op("vector", lambda e, k=k: e.tensor_copy(PT[:, k * S:(k + 1) * S], psum[0:S, 0:S]),
                 reads=[bpsum], writes=[bPT])
            if k <= LC:
                p.op("tensor", lambda e, k=k: e.matmul(
                    psum[0:S, 64:65], A_s, Vm[:, k - 1:k], start=True, stop=True),
                    reads=[bA, bVm], writes=[bpsum])
                p.op("vector", lambda e, k=k: e.tensor_copy(Vm[:, k:k + 1], psum[0:S, 64:65]),
                     reads=[bpsum], writes=[bVm])
        p.op("vector", lambda e: e.tensor_scalar(VmB, Vm[:, 0:LC], Bv_s, None, ALU.mult),
             reads=[bBv, bVm], writes=[bVmB])

    need_m1 = "p1" in phases and DBG_SCONST
    need_m2 = "p2" in phases and DBG_SCONST
    if need_m1:
        build_scan_consts("m1", PTw, Vmat, VmatB, A_sb, Bv_sb, Cv_sb,
                          bPT, bVm, bVmB, bA, bBv, bCv, PS["Ha"], pB["Ha"])
    bsc2 = Buf()
    if need_m2:
        sb.mark()
        PT2 = sb.alloc(S, (LC + 2) * S, F32)
        Vm2 = sb.alloc(S, LC + 1, F32)
        VmB2 = sb.alloc(S, LC, F32)
        A2 = sb.alloc(S, S, F32)
        B2 = sb.alloc(S, 1, F32)
        C2 = sb.alloc(S, 1, F32)
        b2 = [Buf() for _ in range(6)]
        build_scan_consts("m2", PT2, Vm2, VmB2, A2, B2, C2, *b2, PS["Hb"], pB["Hb"])
        p.op("gpsimd", lambda e: e.dma_start(out=sc2[0, :, :], in_=PT2),
             reads=[b2[0]], writes=[bsc2], inc=16)
        p.op("gpsimd", lambda e: e.dma_start(out=sc2[1, :, 0:LC + 1], in_=Vm2),
             reads=[b2[1], bsc2], writes=[bsc2], inc=16)
        p.op("gpsimd", lambda e: e.dma_start(out=sc2[1, :, 256:256 + LC], in_=VmB2),
             reads=[b2[2], bsc2], writes=[bsc2], inc=16)
        sb.reset()
        p.fence()

    def mamba_phase(m, ln_from_x,
                    srcb_fm=None, srcb_hbuf=None, dstb_fm=None, dstb_hbuf=None,
                    wconv_hbuf=None):
        w = mw[m]
        sb.mark()
        in_w_b = sb.alloc(128, 8 * 2048, BF16)
        out_w_b = sb.alloc(128, 8 * 1024, BF16)
        ssm_w_n = sb.alloc(128, 8 * S, BF16)
        ssm_wT_s = sb.alloc(S, D, F32)
        W2b = sb.alloc(128, 8 * LC, BF16)
        bk = sb.alloc(128, 1, F32)
        in_b_pp = sb.alloc(128, 16, F32)
        conv_w_pp = sb.alloc(128, 32, F32)
        conv_b_pp = sb.alloc(128, 8, F32)
        out_b_pp = sb.alloc(128, 8, F32)
        D_pp = sb.alloc(128, 8, F32)
        ssm_b_s = sb.alloc(S, 1, F32)
        zb_s = sb.alloc(S, 1, F32)
        (b_inw, b_outw, b_ssmn, b_ssmT, b_W2, b_bk, b_inb, b_cw, b_cb, b_ob,
         b_D, b_sb, b_zb) = (Buf() for _ in range(13))

        p.group("gpsimd", [
            (lambda e, f=f: e.dma_start(out=in_w_b[:, f * 2048:(f + 1) * 2048],
                                        in_=w["in_w"][f * 128:(f + 1) * 128, :]))
            for f in range(8)], writes=[b_inw], inc_last=16 * 8)
        p.group("gpsimd", [
            (lambda e, f=f: e.dma_start(out=out_w_b[:, f * 1024:(f + 1) * 1024],
                                        in_=w["out_w"][f * 128:(f + 1) * 128, :]))
            for f in range(8)], writes=[b_outw], inc_last=16 * 8)
        p.group("gpsimd", [
            (lambda e, f=f: e.dma_start(out=ssm_w_n[:, f * S:(f + 1) * S],
                                        in_=w["ssm_w"][f * 128:(f + 1) * 128, :]))
            for f in range(8)], writes=[b_ssmn], inc_last=16 * 8)
        p.op("sync", lambda e: e.dma_start(out=ssm_wT_s, in_=w["ssm_wT"]), writes=[b_ssmT], inc=16)
        p.op("sync", lambda e: e.dma_start(out=in_b_pp, in_=w["in_b"]), writes=[b_inb], inc=16)
        p.op("sync", lambda e: e.dma_start(out=conv_w_pp, in_=w["conv_w"]), writes=[b_cw], inc=16)
        p.op("sync", lambda e: e.dma_start(out=conv_b_pp, in_=w["conv_b"]), writes=[b_cb], inc=16)
        p.op("sync", lambda e: e.dma_start(out=out_b_pp, in_=w["out_b"]), writes=[b_ob], inc=16)
        p.op("sync", lambda e: e.dma_start(out=D_pp, in_=w["Dv"]), writes=[b_D], inc=16)
        p.op("sync", lambda e: e.dma_start(out=ssm_b_s, in_=w["ssm_b"]), writes=[b_sb], inc=16)
        if m == "m2":
            p.op("sync", lambda e: e.dma_start(out=PTw, in_=sc2[0, :, :]),
                 reads=[bsc2], writes=[bPT], inc=16)
            p.op("sync", lambda e: e.dma_start(out=Vmat, in_=sc2[1, :, 0:LC + 1]),
                 reads=[bsc2], writes=[bVm], inc=16)
            p.op("sync", lambda e: e.dma_start(out=VmatB, in_=sc2[1, :, 256:256 + LC]),
                 reads=[bsc2], writes=[bVmB], inc=16)
            p.op("sync", lambda e: e.dma_start(out=Bv_sb, in_=w["Bv"]), writes=[bBv], inc=16)
        p.op("vector", lambda e: e.tensor_mul(zb_s, Bv_sb, ssm_b_s),
             reads=[bBv, b_sb], writes=[b_zb])
        for f in range(8):
            p.op("tensor", lambda e, f=f: e.matmul(
                PS["Ha"][:, 0:LC], ssm_wT_s[:, f * 128:(f + 1) * 128], VmatB,
                start=True, stop=True), reads=[b_ssmT, bVmB], writes=[pB["Ha"]])
            p.op("vector", lambda e, f=f: e.tensor_copy(W2b[:, f * LC:(f + 1) * LC], PS["Ha"][:, 0:LC]),
                 reads=[pB["Ha"]], writes=[b_W2])
        p.op("tensor", lambda e: e.matmul(PS["Ha"][:, 256:257], VmatB, ssm_b_s,
                                          start=True, stop=True),
             reads=[bVmB, b_sb], writes=[pB["Ha"]])
        p.op("scalar", lambda e: e.copy(bk, PS["Ha"][:, 256:257]), reads=[pB["Ha"]], writes=[b_bk])

        xin_b = [sb.alloc(128, 8 * TT, BF16) for _ in range(2)]
        b_xin = [Buf(), Buf()]
        xs_buf = sb.alloc(128, 8 * (TT + 4), BF16)
        b_xs = Buf()
        acc1 = sb.alloc(128, TT, BF16)
        acc2 = sb.alloc(128, TT, BF16)
        b_a1, b_a2 = Buf(), Buf()
        u_t = sb.alloc(128, 8 * TT, BF16)
        b_u = Buf()
        sg_t = acc1
        b_sg = b_a1
        y_t = [sb.alloc(128, 8 * TT, BF16) for _ in range(2)]
        b_y = [Buf(), Buf()]
        wv_t = [y_t[1][:, 0:TT], y_t[1][:, TT:2 * TT]]
        b_wv = [b_y[1], b_y[1]]
        ysta = [sb.alloc(128, 2 * TT, BF16) for _ in range(2)]
        b_ysta = [Buf(), Buf()]
        ysk = [sb.alloc(128, YPW, BF16) for _ in range(2)]
        b_ysk = [Buf(), Buf()]
        b_yh = Buf(nowaw=True)
        z_sb = sb.alloc(S, NSEQ * T, BF16)
        b_z = Buf()
        q_sb = sb.alloc(S, NSEQ * NCH, F32)
        b_q = Buf()
        hall = sb.alloc(S, (NCH + 1) * NSEQ, F32)
        b_h = Buf()
        hallT_sb = sb.alloc(S, NSEQ * NCH, F32)
        b_hT = Buf()
        ccr_b = sb.alloc(NSEQ * NCH, LC, BF16)
        b_ccr = Buf()
        c_fl = u_t[0:1, 0:T]
        b_cfl = b_u
        cc_fl = u_t[0:1, T:2 * T]
        b_ccfl = b_u
        p.op("vector", lambda e: e.tensor_copy(PTb16, PTw[:, 0:LC * S]),
             reads=[bPT], writes=[b_PTb])
        cB = sb.alloc(128, T, BF16)
        b_cB = Buf()
        b_ch = Buf()
        res_t = [sb.alloc(128, TT, BF16) for _ in range(2)]
        b_res = [Buf(), Buf()]
        ost16_t = [sb.alloc(128, TT, BF16) for _ in range(2)]
        b_o16 = [Buf(), Buf()]
        wld_t = [sb.alloc(128, TT, BF16) for _ in range(2)]
        b_wld = [Buf(), Buf()]
        if wconv_hbuf is not None:
            wraw = [sb.alloc(128, 1024, I32) for _ in range(2)]
            b_wraw = [Buf(), Buf()]
            wc16 = [sb.alloc(128, 1024, BF16) for _ in range(2)]
            b_wc16 = [Buf(), Buf()]
            wconv_jobs = []
            for wsrc, wdst, nt, n, qz in ((wgT, wg16, 8, H3, True), (wuT, wu16, 8, H3, True),
                                          (wdT, wd16, 24, D, True),
                                          (c1wT, c1w16, 8, 3 * 512, False),
                                          (c4wT, c4w16, 8, 512, False),
                                          (c5wT, c5w16, 8, 3 * D, False)):
                for f in range(nt):
                    for p0 in range(0, n, 1024):
                        wdt = min(1024, n - p0)
                        wconv_jobs.append((wsrc[f * 128:(f + 1) * 128, p0:p0 + wdt],
                                           wdst[f * 128:(f + 1) * 128, p0:p0 + wdt],
                                           wdt, qz))
            wconv_jobs.reverse()

        def emit_wconv(njobs):
            for j in range(njobs):
                if not wconv_jobs:
                    return
                srcap, dstap, wdt, qz = wconv_jobs.pop()
                ri = len(wconv_jobs) % 2
                rwf = wraw[ri] if qz else wraw[ri].bitcast(F32)
                rw = rwf[:, 0:wdt]
                bw16 = wc16[ri][:, 0:wdt]
                p.op("sync", lambda e, srcap=srcap, rw=rw: e.dma_start(out=rw, in_=srcap),
                     writes=[b_wraw[ri]], inc=16)
                eng = ("vector", "scalar")[j % 2]
                if eng == "vector":
                    if qz:
                        p.op("vector", lambda e, rw=rw, bw16=bw16: e.tensor_scalar(
                            bw16, rw, float(1.0 / 127.0), None, ALU.mult),
                            reads=[b_wraw[ri]], writes=[b_wc16[ri]])
                    else:
                        p.op("vector", lambda e, rw=rw, bw16=bw16: e.tensor_copy(bw16, rw),
                             reads=[b_wraw[ri]], writes=[b_wc16[ri]])
                else:
                    if qz:
                        p.op("scalar", lambda e, rw=rw, bw16=bw16: e.mul(bw16, rw, cst[:, 3:4]),
                             reads=[b_wraw[ri], b_cst], writes=[b_wc16[ri]])
                    else:
                        p.op("scalar", lambda e, rw=rw, bw16=bw16: e.copy(bw16, rw),
                             reads=[b_wraw[ri]], writes=[b_wc16[ri]])
                p.op("gpsimd", lambda e, dstap=dstap, bw16=bw16: e.dma_start(
                    out=dstap, in_=bw16),
                    reads=[b_wc16[ri]], writes=[wconv_hbuf], inc=16)
        b_wh = Buf(nowaw=True)
        b_srcb = srcb_hbuf
        b_dstb = dstb_hbuf
        if ln_from_x:
            x_tm = [ysk[1][:, 0:2 * D].bitcast(F32)[:, 0:D],
                    ysk[1][:, 2 * D:4 * D].bitcast(F32)[:, 0:D]]
            b_xtm = [b_ysk[1], b_ysk[1]]
            xn_tm = [sb.alloc(128, D, F32) for _ in range(4)]
            b_xntm = [Buf() for _ in range(4)]
            xc_t = ysk[0][:, 0:2 * D].bitcast(F32)[:, 0:D]
            b_xc = b_ysk[0]
            junk = y_t[0][:, 0:2 * D].bitcast(F32)[:, 0:D]
            b_junk = b_y[0]
            nmean = sb.alloc(128, 4, F32)
            ssq = sb.alloc(128, 4, F32)
            sdv = sb.alloc(128, 4, F32)
            rinv = sb.alloc(128, 4, F32)
            b_nm, b_sq, b_sd, b_ri = Buf(), Buf(), Buf(), Buf()

            def emit_ln(b, tt):
                for tb4 in range(4):
                    tb = tt * 4 + tb4
                    xr = x_tm[tb4 % 2]
                    bxr = b_xtm[tb4 % 2]
                    p.op("sync", lambda e, b=b, tb=tb, xr=xr: e.dma_start(
                        out=xr, in_=x_in[b, tb * 128:(tb + 1) * 128, :]),
                        writes=[bxr], inc=16)
                    p.op("vector", lambda e, xr=xr, tb4=tb4: e.tensor_reduce(
                        nmean[:, tb4:tb4 + 1], xr, mybir.AxisListType.X, ALU.add,
                        negate=True), reads=[bxr], writes=[b_nm])
                    p.op("scalar", lambda e, tb4=tb4: e.mul(
                        nmean[:, tb4:tb4 + 1], nmean[:, tb4:tb4 + 1], cst[:, 2:3]),
                        reads=[b_nm, b_cst], writes=[b_nm])
                    p.op("scalar", lambda e, xr=xr, tb4=tb4: e.add(
                        xn_tm[tb4], xr, nmean[:, tb4:tb4 + 1]),
                        reads=[bxr, b_nm], writes=[b_xntm[tb4]])
                    p.op("scalar", lambda e, tb4=tb4: e.activation(
                        junk, xn_tm[tb4], AF.Square, accum_out=ssq[:, tb4:tb4 + 1]),
                        reads=[b_xntm[tb4]], writes=[b_junk, b_sq])
                p.op("scalar", lambda e: e.activation(
                    sdv, ssq, AF.Sqrt, bias=cst[:, 0:1], scale=cst[:, 2:3]),
                    reads=[b_sq, b_cst], writes=[b_sd])
                p.op("vector", lambda e: e.reciprocal(rinv, sdv),
                     reads=[b_sd], writes=[b_ri])
                for tb4 in range(4):
                    p.op("scalar", lambda e, tb4=tb4: e.mul(
                        xn_tm[tb4], xn_tm[tb4], rinv[:, tb4:tb4 + 1]),
                        reads=[b_ri, b_xntm[tb4]], writes=[b_xntm[tb4]])

        for r in range(2):
            p.op("vector", lambda e, r=r: e.memset(ysta[r][:, :], 0.0), writes=[b_ysta[r]])

        for b in range(NSEQ):
            for tt in range(DBG_NTT if DBG_NTT is not None else NTT):
                ri = tt % 2
                xin = xin_b[ri]
                bx = b_xin[ri]
                if ln_from_x:
                    if DBG_STAGE < 1:
                        continue
                    if tt == 0 and b == 0:
                        emit_ln(0, 0)
                    for f in range(8):
                        if DBG_STAGE < 2:
                            break
                        p.group("tensor", [
                            (lambda e, f=f, tb4=tb4: e.transpose(
                                PS["C"][:, tb4 * 128:(tb4 + 1) * 128],
                                xn_tm[tb4][:, f * 128:(f + 1) * 128], ident))
                            for tb4 in range(4)],
                            reads=[b_xntm[0], b_xntm[1], b_xntm[2], b_xntm[3], b_ident],
                            writes=[pB["C"]])
                        p.op("vector", lambda e, f=f, xin=xin: e.tensor_copy(
                            xin[:, f * TT:(f + 1) * TT], PS["C"][:, 0:512]),
                            reads=[pB["C"]], writes=[bx])
                    g = b * NTT + tt + 1
                    if g < NSEQ * NTT:
                        emit_ln(g // NTT, g % NTT)
                else:
                    p.group("sync", [
                        (lambda e, b=b, f=f, tt=tt, xin=xin: e.dma_start(
                            out=xin[:, f * TT:(f + 1) * TT],
                            in_=srcb_fm[b, f * 128:(f + 1) * 128, tt * TT:(tt + 1) * TT]))
                        for f in range(8)],
                        reads=[b_srcb], writes=[bx], inc_last=16 * 8)

                if tt == 0:
                    p.group("vector", [
                        (lambda e, f=f: e.memset(xs_buf[:, f * (TT + 4):f * (TT + 4) + 4], 0.0))
                        for f in range(8)], writes=[b_xs])
                if DBG_STAGE < 3:
                    continue

                for o in range(16):
                    pb = ("A", "B")[o % 2]
                    p.group("tensor", [
                        (lambda e, o=o, k=k, xin=xin, pb=pb: e.matmul(
                            PS[pb], in_w_b[:, k * 2048 + o * 128:k * 2048 + (o + 1) * 128],
                            xin[:, k * TT:(k + 1) * TT], start=(k == 0), stop=(k == 7)))
                        for k in range(8)],
                        reads=[b_inw, bx], writes=[pB[pb]])
                    if o >= 8 and DBG_STAGE < 5:
                        continue
                    if o < 8:
                        f = o
                        if DBG_STAGE < 4:
                            continue
                        base = f * (TT + 4)
                        p.op("scalar", lambda e, base=base, pb=pb, f=f: e.add(
                            xs_buf[:, base + 4:base + 4 + TT], PS[pb],
                            in_b_pp[:, f:f + 1]),
                            reads=[pB[pb], b_inb], writes=[b_xs])
                        p.op("vector", lambda e, base=base, f=f: e.tensor_scalar(
                            acc1, xs_buf[:, base + 1:base + 1 + TT],
                            conv_w_pp[:, 4 * f:4 * f + 1], None, ALU.mult),
                            reads=[b_xs, b_cw], writes=[b_a1])
                        p.op("vector", lambda e, base=base, f=f: e.scalar_tensor_tensor(
                            acc2, xs_buf[:, base + 2:base + 2 + TT],
                            conv_w_pp[:, 4 * f + 1:4 * f + 2], acc1, ALU.mult, ALU.add),
                            reads=[b_xs, b_cw, b_a1], writes=[b_a2])
                        p.op("vector", lambda e, base=base, f=f: e.scalar_tensor_tensor(
                            acc1, xs_buf[:, base + 3:base + 3 + TT],
                            conv_w_pp[:, 4 * f + 2:4 * f + 3], acc2, ALU.mult, ALU.add),
                            reads=[b_xs, b_cw, b_a2], writes=[b_a1])
                        p.op("vector", lambda e, base=base, f=f: e.scalar_tensor_tensor(
                            acc2, xs_buf[:, base + 4:base + 4 + TT],
                            conv_w_pp[:, 4 * f + 3:4 * f + 4], acc1, ALU.mult, ALU.add),
                            reads=[b_xs, b_cw, b_a1], writes=[b_a2])
                        p.op("scalar", lambda e, f=f: e.activation(
                            u_t[:, f * TT:(f + 1) * TT], acc2, AF.Silu,
                            bias=conv_b_pp[:, f:f + 1]),
                            reads=[b_a2, b_cb], writes=[b_u])
                        p.op("vector", lambda e, base=base: e.tensor_copy(
                            xs_buf[:, base:base + 4], xs_buf[:, base + TT:base + TT + 4]),
                            reads=[b_xs], writes=[b_xs])
                    else:
                        f = o - 8
                        p.op("scalar", lambda e, f=f, pb=pb: e.activation(
                            sg_t, PS[pb], AF.Silu, bias=in_b_pp[:, 8 + f:8 + f + 1]),
                            reads=[pB[pb], b_inb], writes=[b_sg])
                        wb = wv_t[f % 2]
                        bw = b_wv[f % 2]
                        p.op("vector", lambda e, f=f, wb=wb: e.tensor_mul(
                            wb, u_t[:, f * TT:(f + 1) * TT], sg_t),
                            reads=[b_u, b_sg], writes=[bw])
                        p.op("gpsimd", lambda e, b=b, f=f, tt=tt, wb=wb: e.dma_start(
                            out=w_h[b, f * 128:(f + 1) * 128, tt * TT:(tt + 1) * TT], in_=wb),
                            reads=[bw], writes=[b_wh], inc=16)

                if ln_from_x:
                    p.op("gpsimd", lambda e, b=b, tt=tt, xin=xin: e.dma_start(
                        out=xnb_h[b, :, tt * TT:(tt + 1) * TT].rearrange(
                            "(f p) n -> f p n", p=128).transpose([1, 0, 2]),
                        in_=xin.rearrange("p (f n) -> p f n", n=TT)),
                        reads=[bx], writes=[b_srcb], inc=16)
                if DBG_STAGE < 6:
                    continue
                p.group("tensor", [
                    (lambda e, k=k: e.matmul(
                        PS["Ga"][0:S, 0:TT], ssm_w_n[:, k * S:(k + 1) * S],
                        u_t[:, k * TT:(k + 1) * TT], start=(k == 0), stop=(k == 7)))
                    for k in range(8)], reads=[b_ssmn, b_u], writes=[pB["Ga"]])
                p.op("vector", lambda e, b=b, tt=tt: e.tensor_scalar(
                    z_sb[:, b * T + tt * TT:b * T + (tt + 1) * TT], PS["Ga"][0:S, 0:TT],
                    Bv_sb, zb_s, ALU.mult, ALU.add), reads=[pB["Ga"], bBv, b_zb], writes=[b_z])
                p.group("tensor", [
                    (lambda e, k=k: e.matmul(
                        PS["D"], W2b[:, k * LC:(k + 1) * LC], u_t[:, k * TT:(k + 1) * TT],
                        start=(k == 0), stop=(k == 7)))
                    for k in range(8)], reads=[b_W2, b_u], writes=[pB["D"]])
                if DBG_STAGE < 7:
                    continue
                ys = ysta[tt % 2]
                ysdst = ys.rearrange("p (a n) -> p a n", n=256)[:, :, 128:256]
                p.op("scalar", lambda e, ysdst=ysdst: e.add(
                    ysdst, PS["D"].rearrange("p (a n) -> p a n", n=128), bk),
                    reads=[pB["D"], b_bk], writes=[b_ysta[tt % 2]])
                yh_dst = bass.AP(y_hbm.tensor, b * 128 * YPW + tt * 4 * 256,
                                 [[YPW, 128], [1, 1024]])
                p.op("gpsimd", lambda e, ys=ys, yh_dst=yh_dst: e.dma_start(out=yh_dst, in_=ys),
                    reads=[b_ysta[tt % 2]], writes=[b_yh], inc=16)


        if DBG_PASSB:
            p.group("tensor", [
                (lambda e, sig=sig: e.matmul(
                    PS["Gb"][0:S, 0:NSEQ * NCH],
                    PTb16[:, (LC - 1 - sig) * S:(LC - sig) * S],
                    z_sb.rearrange("s (b c q) -> s b c q", c=NCH, q=LC)[:, :, :, sig],
                    start=(sig == 0), stop=(sig == LC - 1)))
                for sig in range(LC)], reads=[b_PTb, b_z], writes=[pB["Gb"]])
            p.op("vector", lambda e: e.tensor_copy(q_sb, PS["Gb"][0:S, 0:NSEQ * NCH]),
                 reads=[pB["Gb"]], writes=[b_q])
            p.op("vector", lambda e: e.memset(hall[:, 0:NSEQ], 0.0), reads=[], writes=[b_h])
            for i in range(NCH):
                p.op("tensor", lambda e, i=i: e.matmul(
                    PS["Gb"][0:S, 300:300 + NSEQ], PTw[:, LC * S:(LC + 1) * S],
                    hall[:, i * NSEQ:(i + 1) * NSEQ],
                    start=True, stop=True), reads=[bPT, b_h], writes=[pB["Gb"]])
                p.op("vector", lambda e, i=i: e.tensor_add(
                    hall[:, (i + 1) * NSEQ:(i + 2) * NSEQ], PS["Gb"][0:S, 300:300 + NSEQ],
                    q_sb[:, i::NCH]),
                    reads=[pB["Gb"], b_q], writes=[b_h])
            hallT = hall.rearrange("s (i b) -> s i b", b=NSEQ)[:, 0:NCH, :].transpose([0, 2, 1])
            p.op("vector", lambda e, hallT=hallT: e.tensor_copy(hallT_sb, hallT),
                 reads=[b_h], writes=[b_hT])
            p.op("tensor", lambda e: e.matmul(
                PS["Gb"][0:NSEQ * NCH, 320:320 + LC], hallT_sb, Vmat[:, 1:LC + 1],
                start=True, stop=True), reads=[b_hT, bVm], writes=[pB["Gb"]])
            p.op("vector", lambda e: e.tensor_copy(ccr_b, PS["Gb"][0:NSEQ * NCH, 320:320 + LC]),
                 reads=[pB["Gb"]], writes=[b_ccr])

        for b in range(NSEQ):
            if not DBG_PASSB:
                break
            p.op("gpsimd", lambda e, b=b: e.dma_start(out=cc_fl, in_=ccr_b[b * NCH:(b + 1) * NCH, :]),
                 reads=[b_ccr], writes=[b_ccfl], inc=16)
            yr = b % 2
            ysk_src = bass.AP(y_hbm.tensor, b * 128 * YPW, [[YPW - 1, 128], [1, YPW]])
            p.op("sync", lambda e, ysk_src=ysk_src, yr=yr: e.dma_start(
                out=ysk[yr], in_=ysk_src),
                reads=[b_yh], writes=[b_ysk[yr]], inc=16)
            ydata = ysk[yr].rearrange("p (c q) -> p c q", q=256)[:, :, 128:256]
            for nt in range(4):
                p.op("tensor", lambda e, nt=nt, ydata=ydata: e.matmul(
                    PS["D"][0:1, 0:512],
                    onesc_b, ydata[:, nt * 4:(nt + 1) * 4, :],
                    start=True, stop=True),
                    reads=[b_onesb, b_ysk[yr]], writes=[pB["D"]])
                p.op("vector", lambda e, nt=nt: e.tensor_add(
                    c_fl[:, nt * 512:(nt + 1) * 512], PS["D"][0:1, 0:512],
                    cc_fl[:, nt * 512:(nt + 1) * 512]),
                    reads=[pB["D"], b_ccfl], writes=[b_cfl])
            p.op("gpsimd", lambda e, b=b: e.dma_start(out=c_h[b:b + 1, :], in_=c_fl),
                 reads=[b_cfl], writes=[b_ch], inc=16)

        for b in range(NSEQ):
            if not DBG_PASSB:
                break
            p.op("gpsimd", lambda e, b=b: e.dma_start(
                out=cB, in_=c_h[b:b + 1, :].partition_broadcast(128)),
                reads=[b_ch], writes=[b_cB], inc=16)

            for tt in range(NTT):
                yy = y_t[tt % 2]
                by = b_y[tt % 2]
                for f in range(8):
                    wl = wld_t[f % 2]
                    bw = b_wld[f % 2]
                    p.op("sync", lambda e, b=b, f=f, tt=tt, wl=wl: e.dma_start(
                        out=wl, in_=w_h[b, f * 128:(f + 1) * 128, tt * TT:(tt + 1) * TT]),
                        reads=[b_wh], writes=[bw], inc=16)
                    p.op("vector", lambda e, f=f, tt=tt, wl=wl, yy=yy: e.scalar_tensor_tensor(
                        yy[:, f * TT:(f + 1) * TT], cB[:, tt * TT:(tt + 1) * TT],
                        D_pp[:, f:f + 1], wl, ALU.add, ALU.mult),
                        reads=[bw, b_cB, b_D], writes=[by])
                for o in range(8):
                    ri2 = o % 2
                    pb = ("A", "B")[ri2]
                    rl = res_t[ri2]
                    brl = b_res[ri2]
                    resb_fm = xnb_h if ln_from_x else srcb_fm
                    resb_hb = b_srcb if ln_from_x else b_srcb
                    p.op("sync", lambda e, b=b, o=o, tt=tt, rl=rl, resb_fm=resb_fm: e.dma_start(
                        out=rl, in_=resb_fm[b, o * 128:(o + 1) * 128, tt * TT:(tt + 1) * TT]),
                        reads=[resb_hb], writes=[brl], inc=16)
                    p.group("tensor", [
                        (lambda e, o=o, k=k, pb=pb, yy=yy: e.matmul(
                            PS[pb], out_w_b[:, k * 1024 + o * 128:k * 1024 + (o + 1) * 128],
                            yy[:, k * TT:(k + 1) * TT], start=(k == 0), stop=(k == 7)))
                        for k in range(8)], reads=[b_outw, by], writes=[pB[pb]])
                    o16 = ost16_t[ri2]
                    p.op("vector", lambda e, o=o, pb=pb, rl=rl, o16=o16: e.scalar_tensor_tensor(
                        o16, PS[pb], out_b_pp[:, o:o + 1], rl, ALU.add, ALU.add),
                        reads=[pB[pb], b_ob, brl], writes=[b_o16[ri2]])
                    p.op("gpsimd", lambda e, b=b, o=o, tt=tt, o16=o16: e.dma_start(
                        out=dstb_fm[b, o * 128:(o + 1) * 128, tt * TT:(tt + 1) * TT], in_=o16),
                        reads=[b_o16[ri2]], writes=[b_dstb], inc=16)
            if wconv_hbuf is not None:
                emit_wconv(30)
        sb.reset()

    def conv_phase(srcb_hbuf, dstb_hbuf, w16_hbuf):
        sb.mark()
        c1T = sb.alloc(128, 8 * 1536, BF16)
        c4T = sb.alloc(128, 8 * 512, BF16)
        c5T = sb.alloc(128, 8 * 3072, BF16)
        c1b_s = sb.alloc(128, 4, F32)
        c1b2 = sb.alloc(128, 4, F32)
        xm2b_r = [sb.alloc(128, 8 * (T + 2), BF16) for _ in range(2)]
        ocb = sb.alloc(128, 8 * (T + 2), BF16)
        res_t = [sb.alloc(128, TT, BF16) for _ in range(2)]
        tmp_t = [sb.alloc(128, TT, F32) for _ in range(2)]
        ost16_t = [sb.alloc(128, TT, BF16) for _ in range(2)]
        b_o16 = [Buf(), Buf()]
        (b_c1, b_c4, b_c5, b_cb1, b_cb2, b_oc) = (Buf() for _ in range(6))
        b_xm2_r = [Buf(), Buf()]
        b_res = [Buf(), Buf()]
        b_tmp = [Buf(), Buf()]

        p.group("sync", [
            (lambda e, f=f: e.dma_start(out=c1T[:, f * 1536:(f + 1) * 1536],
                                        in_=c1w16[f * 128:(f + 1) * 128, :])) for f in range(8)],
            reads=[w16_hbuf], writes=[b_c1], inc_last=16 * 8)
        p.group("sync", [
            (lambda e, f=f: e.dma_start(out=c4T[:, f * 512:(f + 1) * 512],
                                        in_=c4w16[f * 128:(f + 1) * 128, :])) for f in range(8)],
            reads=[w16_hbuf], writes=[b_c4], inc_last=16 * 8)
        p.group("sync", [
            (lambda e, f=f: e.dma_start(out=c5T[:, f * 3072:(f + 1) * 3072],
                                        in_=c5w16[f * 128:(f + 1) * 128, :])) for f in range(8)],
            reads=[w16_hbuf], writes=[b_c5], inc_last=16 * 8)
        p.op("sync", lambda e: e.dma_start(out=c1b_s, in_=c1b), writes=[b_cb1], inc=16)
        p.op("vector", lambda e: e.tensor_scalar(c1b2, c1b_s, float(BN), None, ALU.mult),
             reads=[b_cb1], writes=[b_cb2])
        p.group("vector",
                [(lambda e, f=f, r=r: e.memset(xm2b_r[r][:, f * (T + 2):f * (T + 2) + 1], 0.0))
                 for r in range(2) for f in range(8)] +
                [(lambda e, f=f, r=r: e.memset(
                    xm2b_r[r][:, f * (T + 2) + 1 + T:(f + 1) * (T + 2)], 0.0))
                 for r in range(2) for f in range(8)] +
                [(lambda e, f=f: e.memset(ocb[:, f * (T + 2):f * (T + 2) + 1], 0.0))
                 for f in range(8)] +
                [(lambda e, f=f: e.memset(ocb[:, f * (T + 2) + 1 + T:(f + 1) * (T + 2)], 0.0))
                 for f in range(8)],
                writes=[b_xm2_r[0], b_xm2_r[1], b_oc])

        for b in range(NSEQ):
            xm2b = xm2b_r[b % 2]
            b_xm2 = b_xm2_r[b % 2]
            p.group("sync", [
                (lambda e, b=b, f=f, xm2b=xm2b: e.dma_start(
                    out=xm2b[:, f * (T + 2) + 1:f * (T + 2) + 1 + T],
                    in_=xm2b_h[b, f * 128:(f + 1) * 128, :])) for f in range(8)],
                reads=[srcb_hbuf], writes=[b_xm2], inc_last=16 * 8)
            for tt in range(NTT):
                for c in range(4):
                    pb = ("A", "B")[c % 2]
                    fns = []
                    n = 0
                    for kap in range(3):
                        for f in range(8):
                            fns.append(lambda e, c=c, kap=kap, f=f, tt=tt, n=n, pb=pb,
                                        xm2b=xm2b: e.matmul(
                                PS[pb], c1T[:, f * 1536 + kap * 512 + c * 128:
                                            f * 1536 + kap * 512 + (c + 1) * 128],
                                xm2b[:, f * (T + 2) + tt * TT + kap:
                                     f * (T + 2) + tt * TT + kap + TT],
                                start=(n == 0), stop=(n == 23)))
                            n += 1
                    p.group("tensor", fns, reads=[b_c1, b_xm2], writes=[pB[pb]])
                    p.op("scalar", lambda e, c=c, tt=tt, pb=pb: e.activation(
                        ocb[:, c * (T + 2) + 1 + tt * TT:c * (T + 2) + 1 + (tt + 1) * TT],
                        PS[pb], AF.Relu, bias=c1b2[:, c:c + 1], scale=cst[:, 1:2]),
                        reads=[pB[pb], b_cb2, b_cst], writes=[b_oc])
                for c in range(4):
                    pb = ("A", "B")[c % 2]
                    p.group("tensor", [
                        (lambda e, c=c, f=f, tt=tt, pb=pb, xm2b=xm2b: e.matmul(
                            PS[pb], c4T[:, f * 512 + c * 128:f * 512 + (c + 1) * 128],
                            xm2b[:, f * (T + 2) + tt * TT + 1:f * (T + 2) + tt * TT + 1 + TT],
                            start=(f == 0), stop=(f == 7))) for f in range(8)],
                        reads=[b_c4, b_xm2], writes=[pB[pb]])
                    p.op("scalar", lambda e, c=c, tt=tt, pb=pb: e.activation(
                        ocb[:, (4 + c) * (T + 2) + 1 + tt * TT:
                            (4 + c) * (T + 2) + 1 + (tt + 1) * TT],
                        PS[pb], AF.Relu), reads=[pB[pb]], writes=[b_oc])
            for tt in range(NTT):
                for o in range(8):
                    ri2 = o % 2
                    pb = ("A", "B")[ri2]
                    rl = res_t[ri2]
                    p.op("sync", lambda e, b=b, o=o, tt=tt, rl=rl: e.dma_start(
                        out=rl, in_=xm2b_h[b, o * 128:(o + 1) * 128, tt * TT:(tt + 1) * TT]),
                        reads=[srcb_hbuf], writes=[b_res[ri2]], inc=16)
                    fns = []
                    n = 0
                    for kap in range(3):
                        for f in range(8):
                            fns.append(lambda e, o=o, kap=kap, f=f, tt=tt, n=n, pb=pb: e.matmul(
                                PS[pb], c5T[:, f * 3072 + kap * 1024 + o * 128:
                                            f * 3072 + kap * 1024 + (o + 1) * 128],
                                ocb[:, f * (T + 2) + tt * TT + kap:
                                    f * (T + 2) + tt * TT + kap + TT],
                                start=(n == 0), stop=(n == 23)))
                            n += 1
                    p.group("tensor", fns, reads=[b_c5, b_oc], writes=[pB[pb]])
                    tm = tmp_t[ri2]
                    p.op("scalar", lambda e, pb=pb, tm=tm: e.activation(
                        tm, PS[pb], AF.Relu, scale=cst[:, 1:2]),
                        reads=[pB[pb], b_cst], writes=[b_tmp[ri2]])
                    o16 = ost16_t[ri2]
                    p.op("vector", lambda e, tm=tm, rl=rl, o16=o16: e.tensor_add(o16, tm, rl),
                         reads=[b_tmp[ri2], b_res[ri2]], writes=[b_o16[ri2]])
                    p.op("gpsimd", lambda e, b=b, o=o, tt=tt, o16=o16: e.dma_start(
                        out=s16_h[b, o * 128:(o + 1) * 128, tt * TT:(tt + 1) * TT], in_=o16),
                        reads=[b_o16[ri2]], writes=[dstb_hbuf], inc=16)
        sb.reset()

    def ffn_phase(xn_hbuf, s_hbuf, w16_hbuf):
        sb.mark()
        wgb = sb.alloc(128, 8 * H3, BF16)
        wub = sb.alloc(128, 8 * H3, BF16)
        wdb = sb.alloc(128, 24 * D, BF16)
        hh_all = sb.alloc(128, 24 * TT, BF16)
        b_hh = Buf()
        xn_t = sb.alloc(128, 8 * TT, BF16)
        b_xnt = Buf()
        g_t = sb.alloc(128, TT, F32)
        b_g = Buf()
        sgf = sb.alloc(128, 8 * TT, BF16)
        b_sgf = Buf()
        s_t = [sb.alloc(128, TT, BF16) for _ in range(2)]
        b_st = [Buf(), Buf()]
        x_tm2 = [sb.alloc(128, D, F32) for _ in range(2)]
        b_xt = [Buf(), Buf()]
        o_tm = [sb.alloc(128, D, F32) for _ in range(2)]
        b_ot = [Buf(), Buf()]
        b_wg, b_wu, b_wd = Buf(), Buf(), Buf()
        b_out = Buf(nowaw=True)

        for wsrc, wdst, bw, nt, n in ((wg16, wgb, b_wg, 8, H3), (wu16, wub, b_wu, 8, H3),
                                      (wd16, wdb, b_wd, 24, D)):
            p.group("sync", [
                (lambda e, wsrc=wsrc, wdst=wdst, f=f, n=n: e.dma_start(
                    out=wdst[:, f * n:(f + 1) * n], in_=wsrc[f * 128:(f + 1) * 128, :]))
                for f in range(nt)], reads=[w16_hbuf], writes=[bw], inc_last=16 * nt)

        for b in range(NSEQ):
            for tt in range(NTT):
                p.group("sync", [
                    (lambda e, b=b, f=f, tt=tt: e.dma_start(
                        out=xn_t[:, f * TT:(f + 1) * TT],
                        in_=xnb_h[b, f * 128:(f + 1) * 128, tt * TT:(tt + 1) * TT]))
                    for f in range(8)], reads=[xn_hbuf], writes=[b_xnt], inc_last=16 * 8)
                for hc in range(24):
                    p.group("tensor", [
                        (lambda e, hc=hc, k=k: e.matmul(
                            PS["A"], wgb[:, k * H3 + hc * 128:k * H3 + (hc + 1) * 128],
                            xn_t[:, k * TT:(k + 1) * TT], start=(k == 0), stop=(k == 7)))
                        for k in range(8)], reads=[b_wg, b_xnt], writes=[pB["A"]])
                    p.group("tensor", [
                        (lambda e, hc=hc, k=k: e.matmul(
                            PS["B"], wub[:, k * H3 + hc * 128:k * H3 + (hc + 1) * 128],
                            xn_t[:, k * TT:(k + 1) * TT], start=(k == 0), stop=(k == 7)))
                        for k in range(8)], reads=[b_wu, b_xnt], writes=[pB["B"]])
                    p.op("scalar", lambda e: e.activation(g_t, PS["A"], AF.Silu),
                         reads=[pB["A"]], writes=[b_g])
                    p.op("vector", lambda e, hc=hc: e.tensor_mul(
                        hh_all[:, hc * TT:(hc + 1) * TT], g_t, PS["B"]),
                        reads=[b_g, pB["B"]], writes=[b_hh])
                for dc in range(8):
                    pb = ("C", "D")[dc % 2]
                    p.group("tensor", [
                        (lambda e, dc=dc, hc=hc, pb=pb: e.matmul(
                            PS[pb], wdb[:, hc * D + dc * 128:hc * D + (dc + 1) * 128],
                            hh_all[:, hc * TT:(hc + 1) * TT], start=(hc == 0), stop=(hc == 23)))
                        for hc in range(24)], reads=[b_wd, b_hh], writes=[pB[pb]])
                    p.op("scalar", lambda e, pb=pb: e.activation(g_t, PS[pb], AF.Silu),
                         reads=[pB[pb]], writes=[b_g])
                    sl = s_t[dc % 2]
                    p.op("sync", lambda e, b=b, dc=dc, tt=tt, sl=sl: e.dma_start(
                        out=sl, in_=s16_h[b, dc * 128:(dc + 1) * 128, tt * TT:(tt + 1) * TT]),
                        reads=[s_hbuf], writes=[b_st[dc % 2]], inc=16)
                    p.op("vector", lambda e, dc=dc, sl=sl: e.tensor_mul(
                        sgf[:, dc * TT:(dc + 1) * TT], g_t, sl),
                        reads=[b_g, b_st[dc % 2]], writes=[b_sgf])
                for blk in range(TT // 128):
                    half = blk % 2
                    acc = ("G", "H")[half]
                    xr = x_tm2[half]
                    p.op("sync", lambda e, b=b, tt=tt, blk=blk, xr=xr: e.dma_start(
                        out=xr, in_=x_in[b, tt * TT + blk * 128:tt * TT + (blk + 1) * 128, :]),
                        writes=[b_xt[half]], inc=16)
                    psb = PS[acc].bitcast(BF16)
                    p.group("tensor", [
                        (lambda e, dc=dc, blk=blk, psb=psb: e.transpose(
                            psb[:, dc * 128:(dc + 1) * 128],
                            sgf[:, dc * TT + blk * 128:dc * TT + (blk + 1) * 128], ident_b))
                        for dc in range(8)], reads=[b_sgf, b_identb], writes=[pB[acc]])
                    ob = o_tm[half]
                    p.op("vector", lambda e, ob=ob, xr=xr, psb=psb: e.tensor_add(
                        ob, psb[:, 0:D], xr),
                        reads=[pB[acc], b_xt[half]], writes=[b_ot[half]])
                    p.op("gpsimd", lambda e, b=b, tt=tt, blk=blk, ob=ob: e.dma_start(
                        out=out_t[b, tt * TT + blk * 128:tt * TT + (blk + 1) * 128, :], in_=ob),
                        reads=[b_ot[half]], writes=[b_out], inc=16)
        sb.reset()

    hbb = {"xn": Buf(nowaw=True), "xm1": Buf(nowaw=True), "xm2": Buf(nowaw=True),
           "s": Buf(nowaw=True), "w16": Buf(nowaw=True)}
    if "p1" in phases:
        mamba_phase("m1", ln_from_x=True,
                    srcb_hbuf=hbb["xn"], dstb_fm=xm1b_h, dstb_hbuf=hbb["xm1"])
        p.fence()
    if "p2" in phases:
        mamba_phase("m2", ln_from_x=False,
                    srcb_fm=xm1b_h, srcb_hbuf=hbb["xm1"],
                    dstb_fm=xm2b_h, dstb_hbuf=hbb["xm2"],
                    wconv_hbuf=(hbb["w16"] if ("p3" in phases or "p4" in phases) else None))
        p.fence()
    sb.reset()
    if "p3" in phases:
        conv_phase(srcb_hbuf=hbb["xm2"], dstb_hbuf=hbb["s"], w16_hbuf=hbb["w16"])
        p.fence()
    if "p4" in phases:
        ffn_phase(xn_hbuf=hbb["xn"], s_hbuf=hbb["s"], w16_hbuf=hbb["w16"])
    p.raw("sync", lambda e: None,
          waits=[(k, v) for k, v in p.cnt.items() if k != "sync"], inc=0)
    p.emit()
    return nc


def _prep_inputs(kw):
    def pp(v, nchunk):
        return np.ascontiguousarray(np.asarray(v, np.float32).reshape(nchunk, 128).T)
    d = {}
    for m in ("m1", "m2"):
        P = f"L0_attn_{m}_"
        d[f"{m}_in_w"] = np.ascontiguousarray(kw[P + "in_w"], np.float32)
        d[f"{m}_in_b"] = pp(kw[P + "in_b"], 16)
        cw = np.asarray(kw[P + "conv_w"], np.float32).reshape(8, 128, 4)
        d[f"{m}_conv_w"] = np.ascontiguousarray(np.transpose(cw, (1, 0, 2)).reshape(128, 32))
        d[f"{m}_conv_b"] = pp(kw[P + "conv_b"], 8)
        d[f"{m}_ssm_w"] = np.ascontiguousarray(kw[P + "ssm_w"], np.float32)
        d[f"{m}_ssm_wT"] = np.ascontiguousarray(np.asarray(kw[P + "ssm_w"], np.float32).T)
        d[f"{m}_ssm_b"] = np.ascontiguousarray(np.asarray(kw[P + "ssm_b"], np.float32).reshape(S, 1))
        d[f"{m}_A"] = np.ascontiguousarray(kw[P + "A"], np.float32)
        d[f"{m}_B"] = np.ascontiguousarray(np.asarray(kw[P + "B"], np.float32).reshape(S, 1))
        d[f"{m}_C"] = np.ascontiguousarray(np.asarray(kw[P + "C"], np.float32).reshape(S, 1))
        d[f"{m}_D"] = pp(kw[P + "D"], 8)
        d[f"{m}_out_w"] = np.ascontiguousarray(kw[P + "out_w"], np.float32)
        d[f"{m}_out_b"] = pp(kw[P + "out_b"], 8)
    d["c1wT"] = np.ascontiguousarray(
        np.transpose(np.asarray(kw["L0_attn_c1_w"], np.float32), (1, 2, 0)).reshape(D, 3 * 512))
    d["c1b"] = pp(kw["L0_attn_c1_b"], 4)
    d["c4wT"] = np.ascontiguousarray(np.asarray(kw["L0_attn_c4_w"], np.float32)[:, :, 0].T)
    d["c5wT"] = np.ascontiguousarray(
        np.transpose(np.asarray(kw["L0_attn_c5_w"], np.float32), (1, 2, 0)).reshape(D, 3 * D))
    d["wgT"] = np.ascontiguousarray(np.asarray(kw["L0_ffn_w_gate"], np.int32).T)
    d["wuT"] = np.ascontiguousarray(np.asarray(kw["L0_ffn_w_up"], np.int32).T)
    d["wdT"] = np.ascontiguousarray(np.asarray(kw["L0_ffn_w_down"], np.int32).T)
    return d


def kernel(**kw):
    ncores = 8
    x = np.ascontiguousarray(np.asarray(kw["L0_attn_x"], dtype=np.float32))
    shared = _prep_inputs(kw)
    nc = build_program()
    in_maps = []
    for c in range(ncores):
        m = dict(shared)
        m["x"] = np.ascontiguousarray(x[c * NSEQ:(c + 1) * NSEQ])
        in_maps.append(m)
    res = run_bass_kernel_spmd(nc, in_maps, core_ids=list(range(ncores)))
    return np.concatenate([r["out"] for r in res.results], axis=0)


# version 06 (final)
# speedup vs baseline: 1.0721x; 1.0721x over previous
import numpy as np
import concourse.bass as bass
import concourse.mybir as mybir
from concourse.bass_utils import run_bass_kernel_spmd

F32 = mybir.dt.float32
BF16 = mybir.dt.bfloat16
I32 = mybir.dt.int32
AF = mybir.ActivationFunctionType
ALU = mybir.AluOpType

NSEQ = 4
T = 2048
D = 1024
S = 32
LC = 128
NCH = T // LC
TT = 512
NTT = T // TT
H3 = 3072
BN = float(1.0 / np.sqrt(1.0 + 1e-5))
YPW = NCH * 2 * LC

DBG_NTT = None
DBG_STAGE = 99
DBG_PASSB = True
DBG_SCONST = True


def dt_size(dt):
    return {F32: 4, BF16: 2, I32: 4}[dt]


class SBA:
    def __init__(self, nc, total_bytes):
        self.nc = nc
        self.total = total_bytes
        self.arena = nc.alloc_sbuf_tensor("arena", [128, total_bytes // 4], F32)
        self.off = 0
        self.marks = []
        self.peak = 0

    def alloc(self, parts, n, dtype):
        sz = (n * dt_size(dtype) + 31) & ~31
        assert self.off + sz <= self.total, f"SBUF overflow {self.off + sz}"
        a = self.arena.ap()[0:parts, self.off // 4:(self.off + sz) // 4]
        if dtype != F32:
            a = a.bitcast(dtype)
        self.off += sz
        self.peak = max(self.peak, self.off)
        return a[:, 0:n]

    def mark(self):
        self.marks.append(self.off)

    def reset(self):
        self.off = self.marks.pop()


class Buf:
    __slots__ = ("lw", "lr", "nowaw", "excl", "ds", "ds_cls")

    def __init__(self, nowaw=False, excl=False):
        self.lw = None
        self.lr = {}
        self.nowaw = nowaw
        self.excl = excl
        self.ds = None
        self.ds_cls = None


class Prog:
    ENG = ("sync", "tensor", "vector", "scalar", "gpsimd")

    def __init__(self, nc):
        self.nc = nc
        self.steps = {e: [] for e in self.ENG}
        self.semh = {e: nc.alloc_semaphore(name=f"S_{e}") for e in self.ENG}
        self.cnt = {e: 0 for e in self.ENG}
        self.waited = {e: {} for e in self.ENG}
        self.pending_fence = {}
        self.ndsem = 0
        self.free_sems = []
        self.sem_owner = {}

    def bufsem(self, buf, swdge):
        if buf.ds is None:
            if not swdge and self.free_sems:
                buf.ds = self.free_sems.pop()
            else:
                name = f"D{self.ndsem}"
                self.ndsem += 1
                buf.ds = self.nc.alloc_semaphore(name=name)
                self.semh[name] = buf.ds
                self.cnt.setdefault(name, 0)
            buf.ds_cls = swdge
            if not swdge:
                self.sem_owner[buf.ds.name] = buf
        assert buf.ds_cls == swdge, "SWDGE and HWDGE DMAs cannot share a semaphore"
        return buf.ds.name

    def fence(self):
        snap = [(k, v) for k, v in self.cnt.items()]
        for e in self.ENG:
            self.pending_fence[e] = snap
        for name, buf in self.sem_owner.items():
            buf.ds = None
            self.free_sems.append(self.semh[name])
        self.sem_owner = {}

    SELF_ORDERED = ()

    def _resolve(self, eng, waits):
        out = []
        for s, v in waits:
            if s == eng and eng in self.SELF_ORDERED:
                continue
            if v > self.waited[eng].get(s, 0):
                out.append((s, v))
                self.waited[eng][s] = v
        return out

    def raw(self, eng, fn, waits=(), inc=1, inc_key=None):
        waits = list(waits)
        if eng in self.pending_fence:
            waits = list(self.pending_fence.pop(eng)) + waits
        w = self._resolve(eng, waits)
        key = inc_key or eng
        if inc:
            self.cnt[key] = self.cnt.get(key, 0) + inc
        tgt = self.cnt.get(key, 0)
        semh = self.semh

        def step(engobj, w=w, fn=fn, inc=inc, key=key):
            for s, v in w:
                engobj.wait_ge(semh[s], v)
            ins = fn(engobj)
            if inc:
                ins.then_inc(semh[key], inc)
        self.steps[eng].append(step)
        return tgt

    @staticmethod
    def _hazard_waits(reads, writes, extra):
        waits = list(extra)
        for b in reads:
            if b.nowaw:
                waits.extend((b.lw or {}).items())
            elif b.lw:
                waits.append(b.lw)
            if b.excl:
                waits.extend(b.lr.items())
        for b in writes:
            if b.lw and not b.nowaw:
                waits.append(b.lw)
            waits.extend(b.lr.items())
        return waits

    @staticmethod
    def _hazard_record(reads, writes, key, c):
        for b in reads:
            b.lr[key] = c
        for b in writes:
            if b.nowaw:
                if b.lw is None:
                    b.lw = {}
                b.lw[key] = max(b.lw.get(key, 0), c)
            else:
                b.lw = (key, c)
            b.lr = {}

    def _dma_key(self, eng, reads, writes):
        assert len(writes) == 1
        wb = writes[0]
        swdge = eng == "gpsimd"
        if wb.nowaw:
            srcs = [b for b in reads if not b.nowaw]
            assert len(srcs) == 1, "nowaw DMA write needs one serialized source"
            return self.bufsem(srcs[0], swdge)
        return self.bufsem(wb, swdge)

    def op(self, eng, fn, reads=(), writes=(), extra=(), inc=1):
        waits = self._hazard_waits(reads, writes, extra)
        key = self._dma_key(eng, reads, writes) if inc >= 16 else None
        c = self.raw(eng, fn, waits=waits, inc=inc, inc_key=key)
        self._hazard_record(reads, writes, key or eng, c)
        return c

    def group(self, eng, fns, reads=(), writes=(), extra=(), inc_last=1):
        waits = self._hazard_waits(reads, writes, extra)
        isdma = inc_last >= 16
        key = None
        if isdma:
            assert inc_last == 16 * len(fns)
            key = self._dma_key(eng, reads, writes)
        c = None
        for i, fn in enumerate(fns):
            inc = 16 if isdma else (inc_last if i == len(fns) - 1 else 0)
            c = self.raw(eng, fn, waits=(waits if i == 0 else ()), inc=inc, inc_key=key)
        self._hazard_record(reads, writes, key or eng, c)
        return c

    def emit(self):
        nc = self.nc
        with nc.Block() as block:
            @block.sync
            def _(eng):
                for st in self.steps["sync"]:
                    st(eng)

            @block.tensor
            def _(eng):
                for st in self.steps["tensor"]:
                    st(eng)

            @block.vector
            def _(eng):
                for st in self.steps["vector"]:
                    st(eng)

            @block.scalar
            def _(eng):
                for st in self.steps["scalar"]:
                    st(eng)

            @block.gpsimd
            def _(eng):
                for st in self.steps["gpsimd"]:
                    st(eng)


def build_program(dbg=(), phases=("p1", "p2", "p3", "p4")):
    nc = bass.Bass("TRN2", target_bir_lowering=False, debug=False)

    def dram(name, shape, dtype, kind="ExternalInput"):
        if name in dbg and kind == "Internal":
            kind = "ExternalOutput"
        return nc.dram_tensor(name, list(shape), dtype, kind=kind).ap()

    x_in = dram("x", (NSEQ, T, D), F32)
    mw = {}
    for m in ("m1", "m2"):
        mw[m] = dict(
            in_w=dram(f"{m}_in_w", (D, 2 * D), F32),
            in_b=dram(f"{m}_in_b", (128, 16), F32),
            conv_w=dram(f"{m}_conv_w", (128, 32), F32),
            conv_b=dram(f"{m}_conv_b", (128, 8), F32),
            ssm_w=dram(f"{m}_ssm_w", (D, S), F32),
            ssm_wT=dram(f"{m}_ssm_wT", (S, D), F32),
            ssm_b=dram(f"{m}_ssm_b", (S, 1), F32),
            A=dram(f"{m}_A", (S, S), F32),
            Bv=dram(f"{m}_B", (S, 1), F32),
            Cv=dram(f"{m}_C", (S, 1), F32),
            Dv=dram(f"{m}_D", (128, 8), F32),
            out_w=dram(f"{m}_out_w", (D, D), F32),
            out_b=dram(f"{m}_out_b", (128, 8), F32),
        )
    c1wT = dram("c1wT", (D, 3 * 512), F32)
    c1b = dram("c1b", (128, 4), F32)
    c4wT = dram("c4wT", (D, 512), F32)
    c5wT = dram("c5wT", (D, 3 * D), F32)
    wgT = dram("wgT", (D, H3), I32)
    wuT = dram("wuT", (D, H3), I32)
    wdT = dram("wdT", (H3, D), I32)

    xnb_h = dram("xnb_h", (NSEQ, D, T), BF16, kind="Internal")
    xm1b_h = dram("xm1b_h", (NSEQ, D, T), BF16, kind="Internal")
    xm2b_h = dram("xm2b_h", (NSEQ, D, T), BF16, kind="Internal")
    s16_h = dram("s16_h", (NSEQ, D, T), BF16, kind="Internal")
    wg16 = dram("wg16", (D, H3), BF16, kind="Internal")
    c1w16 = dram("c1w16", (D, 3 * 512), BF16, kind="Internal")
    c4w16 = dram("c4w16", (D, 512), BF16, kind="Internal")
    c5w16 = dram("c5w16", (D, 3 * D), BF16, kind="Internal")
    wu16 = dram("wu16", (D, H3), BF16, kind="Internal")
    wd16 = dram("wd16", (H3, D), BF16, kind="Internal")
    w_h = dram("w_h", (NSEQ, D, T), BF16, kind="Internal")
    sc2 = dram("scan2", (2, S, (LC + 2) * S), F32, kind="Internal")
    c_h = dram("c_h", (NSEQ, T), BF16, kind="Internal")
    y_hbm = dram("y_hbm", (NSEQ, 128, YPW), BF16, kind="Internal")
    out_t = dram("out", (NSEQ, T, D), F32, kind="ExternalOutput")

    ident_h = nc.inline_tensor(np.eye(128, dtype=np.float32), name="ident128").ap()
    onesc_h = nc.inline_tensor(np.ones((128, 1), dtype=np.float32), name="ones128").ap()

    p = Prog(nc)
    sb = SBA(nc, 206 * 1024)

    PS = {}
    for nm, wdt in (("A", 512), ("B", 512), ("C", 512), ("D", 512), ("G", 1024),
                    ("H", 1024)):
        PS[nm] = nc.alloc_psum_tensor(f"pb{nm}", [128, wdt], F32).ap()
    PS["Ga"] = PS["G"][:, 0:512]
    PS["Gb"] = PS["G"][:, 512:1024]
    PS["Ha"] = PS["H"][:, 0:512]
    PS["Hb"] = PS["H"][:, 512:1024]
    pB = {nm: Buf(excl=True) for nm in PS}
    pB["G"] = None
    pB["H"] = None
    pBG = Buf(excl=True)
    pBH = Buf(excl=True)
    pB["G"] = pBG
    pB["H"] = pBH

    ident = sb.alloc(128, 128, F32)
    ident_b = sb.alloc(128, 128, BF16)
    b_ident = Buf()
    b_identb = Buf()
    p.op("sync", lambda e: e.dma_start(out=ident, in_=ident_h), writes=[b_ident], inc=16)
    p.op("vector", lambda e: e.tensor_copy(ident_b, ident), reads=[b_ident], writes=[b_identb])
    cst = sb.alloc(128, 4, F32)
    b_cst = Buf()
    onesc = sb.alloc(128, 1, F32)
    onesc_b = sb.alloc(128, 1, BF16)
    b_ones = Buf()
    b_onesb = Buf()
    p.op("sync", lambda e: e.dma_start(out=onesc, in_=onesc_h), writes=[b_ones], inc=16)
    p.op("vector", lambda e: e.tensor_copy(onesc_b, onesc), reads=[b_ones], writes=[b_onesb])
    p.op("vector", lambda e: e.memset(cst[:, 0:1], 1e-5), writes=[b_cst])
    p.op("vector", lambda e: e.memset(cst[:, 1:2], float(BN)), reads=[b_cst], writes=[b_cst])
    p.op("vector", lambda e: e.memset(cst[:, 2:3], float(1.0 / D)), reads=[b_cst], writes=[b_cst])
    p.op("vector", lambda e: e.memset(cst[:, 3:4], float(1.0 / 127.0)), reads=[b_cst], writes=[b_cst])

    sb.mark()
    PTw = sb.alloc(S, (LC + 2) * S, F32)
    Vmat = sb.alloc(S, LC + 1, F32)
    VmatB = sb.alloc(S, LC, F32)
    A_sb = sb.alloc(S, S, F32)
    Bv_sb = sb.alloc(S, 1, F32)
    Cv_sb = sb.alloc(S, 1, F32)
    PTb16 = sb.alloc(S, LC * S, BF16)
    bPT, bVm, bVmB, bA, bBv, bCv = (Buf() for _ in range(6))
    b_PTb = Buf()

    def build_scan_consts(m, PT, Vm, VmB, A_s, Bv_s, Cv_s, bPT, bVm, bVmB, bA, bBv, bCv, psum, bpsum):
        w = mw[m]
        p.op("sync", lambda e: e.dma_start(out=A_s, in_=w["A"]), writes=[bA], inc=16)
        p.op("sync", lambda e: e.dma_start(out=Bv_s, in_=w["Bv"]), writes=[bBv], inc=16)
        p.op("sync", lambda e: e.dma_start(out=Cv_s, in_=w["Cv"]), writes=[bCv], inc=16)
        p.op("vector", lambda e: e.tensor_copy(PT[:, 0:S], ident[0:S, 0:S]),
             reads=[b_ident], writes=[bPT])
        p.op("vector", lambda e: e.tensor_copy(Vm[:, 0:1], Cv_s), reads=[bCv], writes=[bVm])
        for k in range(1, LC + 2):
            p.op("tensor", lambda e, k=k: e.matmul(
                psum[0:S, 0:S], A_s, PT[:, (k - 1) * S:k * S], start=True, stop=True),
                reads=[bA, bPT], writes=[bpsum])
            p.op("vector", lambda e, k=k: e.tensor_copy(PT[:, k * S:(k + 1) * S], psum[0:S, 0:S]),
                 reads=[bpsum], writes=[bPT])
            if k <= LC:
                p.op("tensor", lambda e, k=k: e.matmul(
                    psum[0:S, 64:65], A_s, Vm[:, k - 1:k], start=True, stop=True),
                    reads=[bA, bVm], writes=[bpsum])
                p.op("vector", lambda e, k=k: e.tensor_copy(Vm[:, k:k + 1], psum[0:S, 64:65]),
                     reads=[bpsum], writes=[bVm])
        p.op("vector", lambda e: e.tensor_scalar(VmB, Vm[:, 0:LC], Bv_s, None, ALU.mult),
             reads=[bBv, bVm], writes=[bVmB])

    need_m1 = "p1" in phases and DBG_SCONST
    need_m2 = "p2" in phases and DBG_SCONST
    if need_m1:
        build_scan_consts("m1", PTw, Vmat, VmatB, A_sb, Bv_sb, Cv_sb,
                          bPT, bVm, bVmB, bA, bBv, bCv, PS["Ha"], pB["Ha"])
    bsc2 = Buf()
    if need_m2:
        sb.mark()
        PT2 = sb.alloc(S, (LC + 2) * S, F32)
        Vm2 = sb.alloc(S, LC + 1, F32)
        VmB2 = sb.alloc(S, LC, F32)
        A2 = sb.alloc(S, S, F32)
        B2 = sb.alloc(S, 1, F32)
        C2 = sb.alloc(S, 1, F32)
        b2 = [Buf() for _ in range(6)]
        build_scan_consts("m2", PT2, Vm2, VmB2, A2, B2, C2, *b2, PS["Hb"], pB["Hb"])
        p.op("gpsimd", lambda e: e.dma_start(out=sc2[0, :, :], in_=PT2),
             reads=[b2[0]], writes=[bsc2], inc=16)
        p.op("gpsimd", lambda e: e.dma_start(out=sc2[1, :, 0:LC + 1], in_=Vm2),
             reads=[b2[1], bsc2], writes=[bsc2], inc=16)
        p.op("gpsimd", lambda e: e.dma_start(out=sc2[1, :, 256:256 + LC], in_=VmB2),
             reads=[b2[2], bsc2], writes=[bsc2], inc=16)
        sb.reset()
        p.fence()

    def mamba_phase(m, ln_from_x,
                    srcb_fm=None, srcb_hbuf=None, dstb_fm=None, dstb_hbuf=None,
                    wconv_hbuf=None):
        w = mw[m]
        sb.mark()
        in_w_b = sb.alloc(128, 8 * 2048, BF16)
        out_w_b = sb.alloc(128, 8 * 1024, BF16)
        ssm_w_n = sb.alloc(128, 8 * S, BF16)
        ssm_wT_s = sb.alloc(S, D, F32)
        W2b = sb.alloc(128, 8 * LC, BF16)
        bk = sb.alloc(128, 1, F32)
        in_b_pp = sb.alloc(128, 16, F32)
        conv_w_pp = sb.alloc(128, 32, F32)
        conv_b_pp = sb.alloc(128, 8, F32)
        out_b_pp = sb.alloc(128, 8, F32)
        D_pp = sb.alloc(128, 8, F32)
        ssm_b_s = sb.alloc(S, 1, F32)
        zb_s = sb.alloc(S, 1, F32)
        (b_inw, b_outw, b_ssmn, b_ssmT, b_W2, b_bk, b_inb, b_cw, b_cb, b_ob,
         b_D, b_sb, b_zb) = (Buf() for _ in range(13))

        p.group("gpsimd", [
            (lambda e, f=f: e.dma_start(out=in_w_b[:, f * 2048:(f + 1) * 2048],
                                        in_=w["in_w"][f * 128:(f + 1) * 128, :]))
            for f in range(8)], writes=[b_inw], inc_last=16 * 8)
        p.group("gpsimd", [
            (lambda e, f=f: e.dma_start(out=out_w_b[:, f * 1024:(f + 1) * 1024],
                                        in_=w["out_w"][f * 128:(f + 1) * 128, :]))
            for f in range(8)], writes=[b_outw], inc_last=16 * 8)
        p.group("gpsimd", [
            (lambda e, f=f: e.dma_start(out=ssm_w_n[:, f * S:(f + 1) * S],
                                        in_=w["ssm_w"][f * 128:(f + 1) * 128, :]))
            for f in range(8)], writes=[b_ssmn], inc_last=16 * 8)
        p.op("sync", lambda e: e.dma_start(out=ssm_wT_s, in_=w["ssm_wT"]), writes=[b_ssmT], inc=16)
        p.op("sync", lambda e: e.dma_start(out=in_b_pp, in_=w["in_b"]), writes=[b_inb], inc=16)
        p.op("sync", lambda e: e.dma_start(out=conv_w_pp, in_=w["conv_w"]), writes=[b_cw], inc=16)
        p.op("sync", lambda e: e.dma_start(out=conv_b_pp, in_=w["conv_b"]), writes=[b_cb], inc=16)
        p.op("sync", lambda e: e.dma_start(out=out_b_pp, in_=w["out_b"]), writes=[b_ob], inc=16)
        p.op("sync", lambda e: e.dma_start(out=D_pp, in_=w["Dv"]), writes=[b_D], inc=16)
        p.op("sync", lambda e: e.dma_start(out=ssm_b_s, in_=w["ssm_b"]), writes=[b_sb], inc=16)
        if m == "m2":
            p.op("sync", lambda e: e.dma_start(out=PTw, in_=sc2[0, :, :]),
                 reads=[bsc2], writes=[bPT], inc=16)
            p.op("sync", lambda e: e.dma_start(out=Vmat, in_=sc2[1, :, 0:LC + 1]),
                 reads=[bsc2], writes=[bVm], inc=16)
            p.op("sync", lambda e: e.dma_start(out=VmatB, in_=sc2[1, :, 256:256 + LC]),
                 reads=[bsc2], writes=[bVmB], inc=16)
            p.op("sync", lambda e: e.dma_start(out=Bv_sb, in_=w["Bv"]), writes=[bBv], inc=16)
        p.op("vector", lambda e: e.tensor_mul(zb_s, Bv_sb, ssm_b_s),
             reads=[bBv, b_sb], writes=[b_zb])
        for f in range(8):
            p.op("tensor", lambda e, f=f: e.matmul(
                PS["Ha"][:, 0:LC], ssm_wT_s[:, f * 128:(f + 1) * 128], VmatB,
                start=True, stop=True), reads=[b_ssmT, bVmB], writes=[pB["Ha"]])
            p.op("vector", lambda e, f=f: e.tensor_copy(W2b[:, f * LC:(f + 1) * LC], PS["Ha"][:, 0:LC]),
                 reads=[pB["Ha"]], writes=[b_W2])
        p.op("tensor", lambda e: e.matmul(PS["Ha"][:, 256:257], VmatB, ssm_b_s,
                                          start=True, stop=True),
             reads=[bVmB, b_sb], writes=[pB["Ha"]])
        p.op("scalar", lambda e: e.copy(bk, PS["Ha"][:, 256:257]), reads=[pB["Ha"]], writes=[b_bk])

        xin_b = [sb.alloc(128, 8 * TT, BF16) for _ in range(2)]
        b_xin = [Buf(), Buf()]
        xs_buf = sb.alloc(128, 8 * (TT + 4), BF16)
        b_xs = Buf()
        acc1 = sb.alloc(128, TT, BF16)
        acc2 = sb.alloc(128, TT, BF16)
        b_a1, b_a2 = Buf(), Buf()
        u_t = sb.alloc(128, 8 * TT, BF16)
        b_u = Buf()
        sg_t = acc1
        b_sg = b_a1
        y_t = [sb.alloc(128, 8 * TT, BF16) for _ in range(2)]
        b_y = [Buf(), Buf()]
        wv_t = [y_t[1][:, 0:TT], y_t[1][:, TT:2 * TT]]
        b_wv = [b_y[1], b_y[1]]
        ysta = [sb.alloc(128, 2 * TT, BF16) for _ in range(2)]
        b_ysta = [Buf(), Buf()]
        ysk = [sb.alloc(128, YPW, BF16) for _ in range(2)]
        b_ysk = [Buf(), Buf()]
        b_yh = Buf(nowaw=True)
        z_sb = sb.alloc(S, NSEQ * T, BF16)
        b_z = Buf()
        q_sb = sb.alloc(S, NSEQ * NCH, F32)
        b_q = Buf()
        hall = sb.alloc(S, (NCH + 1) * NSEQ, F32)
        b_h = Buf()
        hallT_sb = sb.alloc(S, NSEQ * NCH, F32)
        b_hT = Buf()
        ccr_b = sb.alloc(NSEQ * NCH, LC, BF16)
        b_ccr = Buf()
        c_fl = u_t[0:1, 0:T]
        b_cfl = b_u
        cc_fl = u_t[0:1, T:2 * T]
        b_ccfl = b_u
        p.op("vector", lambda e: e.tensor_copy(PTb16, PTw[:, 0:LC * S]),
             reads=[bPT], writes=[b_PTb])
        cB = sb.alloc(128, T, BF16)
        b_cB = Buf()
        b_ch = Buf()
        res_t = [sb.alloc(128, TT, BF16) for _ in range(2)]
        b_res = [Buf(), Buf()]
        ost16_t = [sb.alloc(128, TT, BF16) for _ in range(2)]
        b_o16 = [Buf(), Buf()]
        wld_t = [sb.alloc(128, TT, BF16) for _ in range(2)]
        b_wld = [Buf(), Buf()]
        if wconv_hbuf is not None:
            wraw = [sb.alloc(128, 1024, I32) for _ in range(2)]
            b_wraw = [Buf(), Buf()]
            wc16 = [sb.alloc(128, 1024, BF16) for _ in range(2)]
            b_wc16 = [Buf(), Buf()]
            wconv_jobs = []
            for wsrc, wdst, nt, n, qz in ((wgT, wg16, 8, H3, True), (wuT, wu16, 8, H3, True),
                                          (wdT, wd16, 24, D, True),
                                          (c1wT, c1w16, 8, 3 * 512, False),
                                          (c4wT, c4w16, 8, 512, False),
                                          (c5wT, c5w16, 8, 3 * D, False)):
                for f in range(nt):
                    for p0 in range(0, n, 1024):
                        wdt = min(1024, n - p0)
                        wconv_jobs.append((wsrc[f * 128:(f + 1) * 128, p0:p0 + wdt],
                                           wdst[f * 128:(f + 1) * 128, p0:p0 + wdt],
                                           wdt, qz))
            wconv_jobs.reverse()

        def emit_wconv(njobs):
            for j in range(njobs):
                if not wconv_jobs:
                    return
                srcap, dstap, wdt, qz = wconv_jobs.pop()
                ri = len(wconv_jobs) % 2
                rwf = wraw[ri] if qz else wraw[ri].bitcast(F32)
                rw = rwf[:, 0:wdt]
                bw16 = wc16[ri][:, 0:wdt]
                p.op("sync", lambda e, srcap=srcap, rw=rw: e.dma_start(out=rw, in_=srcap),
                     writes=[b_wraw[ri]], inc=16)
                eng = ("vector", "scalar")[j % 2]
                if eng == "vector":
                    if qz:
                        p.op("vector", lambda e, rw=rw, bw16=bw16: e.tensor_scalar(
                            bw16, rw, float(1.0 / 127.0), None, ALU.mult),
                            reads=[b_wraw[ri]], writes=[b_wc16[ri]])
                    else:
                        p.op("vector", lambda e, rw=rw, bw16=bw16: e.tensor_copy(bw16, rw),
                             reads=[b_wraw[ri]], writes=[b_wc16[ri]])
                else:
                    if qz:
                        p.op("scalar", lambda e, rw=rw, bw16=bw16: e.mul(bw16, rw, cst[:, 3:4]),
                             reads=[b_wraw[ri], b_cst], writes=[b_wc16[ri]])
                    else:
                        p.op("scalar", lambda e, rw=rw, bw16=bw16: e.copy(bw16, rw),
                             reads=[b_wraw[ri]], writes=[b_wc16[ri]])
                p.op("gpsimd", lambda e, dstap=dstap, bw16=bw16: e.dma_start(
                    out=dstap, in_=bw16),
                    reads=[b_wc16[ri]], writes=[wconv_hbuf], inc=16)
        b_wh = Buf(nowaw=True)
        b_srcb = srcb_hbuf
        b_dstb = dstb_hbuf
        if ln_from_x:
            x_tm = [ysk[1][:, 0:2 * D].bitcast(F32)[:, 0:D],
                    ysk[1][:, 2 * D:4 * D].bitcast(F32)[:, 0:D]]
            b_xtm = [b_ysk[1], b_ysk[1]]
            xn_tm = [sb.alloc(128, D, F32) for _ in range(4)]
            b_xntm = [Buf() for _ in range(4)]
            xc_t = ysk[0][:, 0:2 * D].bitcast(F32)[:, 0:D]
            b_xc = b_ysk[0]
            junk = y_t[0][:, 0:2 * D].bitcast(F32)[:, 0:D]
            b_junk = b_y[0]
            nmean = sb.alloc(128, 4, F32)
            ssq = sb.alloc(128, 4, F32)
            sdv = sb.alloc(128, 4, F32)
            rinv = sb.alloc(128, 4, F32)
            b_nm, b_sq, b_sd, b_ri = Buf(), Buf(), Buf(), Buf()

            def emit_ln(b, tt):
                for tb4 in range(4):
                    tb = tt * 4 + tb4
                    xr = x_tm[tb4 % 2]
                    bxr = b_xtm[tb4 % 2]
                    p.op("sync", lambda e, b=b, tb=tb, xr=xr: e.dma_start(
                        out=xr, in_=x_in[b, tb * 128:(tb + 1) * 128, :]),
                        writes=[bxr], inc=16)
                    p.op("vector", lambda e, xr=xr, tb4=tb4: e.tensor_reduce(
                        nmean[:, tb4:tb4 + 1], xr, mybir.AxisListType.X, ALU.add,
                        negate=True), reads=[bxr], writes=[b_nm])
                    p.op("scalar", lambda e, tb4=tb4: e.mul(
                        nmean[:, tb4:tb4 + 1], nmean[:, tb4:tb4 + 1], cst[:, 2:3]),
                        reads=[b_nm, b_cst], writes=[b_nm])
                    p.op("scalar", lambda e, xr=xr, tb4=tb4: e.add(
                        xn_tm[tb4], xr, nmean[:, tb4:tb4 + 1]),
                        reads=[bxr, b_nm], writes=[b_xntm[tb4]])
                    p.op("scalar", lambda e, tb4=tb4: e.activation(
                        junk, xn_tm[tb4], AF.Square, accum_out=ssq[:, tb4:tb4 + 1]),
                        reads=[b_xntm[tb4]], writes=[b_junk, b_sq])
                p.op("scalar", lambda e: e.activation(
                    sdv, ssq, AF.Sqrt, bias=cst[:, 0:1], scale=cst[:, 2:3]),
                    reads=[b_sq, b_cst], writes=[b_sd])
                p.op("vector", lambda e: e.reciprocal(rinv, sdv),
                     reads=[b_sd], writes=[b_ri])
                for tb4 in range(4):
                    p.op("scalar", lambda e, tb4=tb4: e.mul(
                        xn_tm[tb4], xn_tm[tb4], rinv[:, tb4:tb4 + 1]),
                        reads=[b_ri, b_xntm[tb4]], writes=[b_xntm[tb4]])

        for r in range(2):
            p.op("vector", lambda e, r=r: e.memset(ysta[r][:, :], 0.0), writes=[b_ysta[r]])

        pending_zy = None
        for b in range(NSEQ):
            for tt in range(DBG_NTT if DBG_NTT is not None else NTT):
                ri = tt % 2
                xin = xin_b[ri]
                bx = b_xin[ri]
                if ln_from_x:
                    if DBG_STAGE < 1:
                        continue
                    if tt == 0 and b == 0:
                        emit_ln(0, 0)
                    for f in range(8):
                        if DBG_STAGE < 2:
                            break
                        p.group("tensor", [
                            (lambda e, f=f, tb4=tb4: e.transpose(
                                PS["C"][:, tb4 * 128:(tb4 + 1) * 128],
                                xn_tm[tb4][:, f * 128:(f + 1) * 128], ident))
                            for tb4 in range(4)],
                            reads=[b_xntm[0], b_xntm[1], b_xntm[2], b_xntm[3], b_ident],
                            writes=[pB["C"]])
                        p.op("vector", lambda e, f=f, xin=xin: e.tensor_copy(
                            xin[:, f * TT:(f + 1) * TT], PS["C"][:, 0:512]),
                            reads=[pB["C"]], writes=[bx])
                    g = b * NTT + tt + 1
                    if g < NSEQ * NTT:
                        emit_ln(g // NTT, g % NTT)
                else:
                    p.group("sync", [
                        (lambda e, b=b, f=f, tt=tt, xin=xin: e.dma_start(
                            out=xin[:, f * TT:(f + 1) * TT],
                            in_=srcb_fm[b, f * 128:(f + 1) * 128, tt * TT:(tt + 1) * TT]))
                        for f in range(8)],
                        reads=[b_srcb], writes=[bx], inc_last=16 * 8)

                if tt == 0:
                    p.group("vector", [
                        (lambda e, f=f: e.memset(xs_buf[:, f * (TT + 4):f * (TT + 4) + 4], 0.0))
                        for f in range(8)], writes=[b_xs])
                if DBG_STAGE < 3:
                    continue

                for o in range(16):
                    pb = ("A", "B")[o % 2]
                    p.group("tensor", [
                        (lambda e, o=o, k=k, xin=xin, pb=pb: e.matmul(
                            PS[pb], in_w_b[:, k * 2048 + o * 128:k * 2048 + (o + 1) * 128],
                            xin[:, k * TT:(k + 1) * TT], start=(k == 0), stop=(k == 7)))
                        for k in range(8)],
                        reads=[b_inw, bx], writes=[pB[pb]])
                    if o >= 8 and DBG_STAGE < 5:
                        continue
                    if o < 8:
                        f = o
                        if DBG_STAGE < 4:
                            continue
                        base = f * (TT + 4)
                        p.op("scalar", lambda e, base=base, pb=pb, f=f: e.add(
                            xs_buf[:, base + 4:base + 4 + TT], PS[pb],
                            in_b_pp[:, f:f + 1]),
                            reads=[pB[pb], b_inb], writes=[b_xs])
                        p.op("vector", lambda e, base=base, f=f: e.tensor_scalar(
                            acc1, xs_buf[:, base + 1:base + 1 + TT],
                            conv_w_pp[:, 4 * f:4 * f + 1], None, ALU.mult),
                            reads=[b_xs, b_cw], writes=[b_a1])
                        p.op("vector", lambda e, base=base, f=f: e.scalar_tensor_tensor(
                            acc2, xs_buf[:, base + 2:base + 2 + TT],
                            conv_w_pp[:, 4 * f + 1:4 * f + 2], acc1, ALU.mult, ALU.add),
                            reads=[b_xs, b_cw, b_a1], writes=[b_a2])
                        p.op("vector", lambda e, base=base, f=f: e.scalar_tensor_tensor(
                            acc1, xs_buf[:, base + 3:base + 3 + TT],
                            conv_w_pp[:, 4 * f + 2:4 * f + 3], acc2, ALU.mult, ALU.add),
                            reads=[b_xs, b_cw, b_a2], writes=[b_a1])
                        p.op("vector", lambda e, base=base, f=f: e.scalar_tensor_tensor(
                            acc2, xs_buf[:, base + 4:base + 4 + TT],
                            conv_w_pp[:, 4 * f + 3:4 * f + 4], acc1, ALU.mult, ALU.add),
                            reads=[b_xs, b_cw, b_a1], writes=[b_a2])
                        p.op("scalar", lambda e, f=f: e.activation(
                            u_t[:, f * TT:(f + 1) * TT], acc2, AF.Silu,
                            bias=conv_b_pp[:, f:f + 1]),
                            reads=[b_a2, b_cb], writes=[b_u])
                        p.op("vector", lambda e, base=base: e.tensor_copy(
                            xs_buf[:, base:base + 4], xs_buf[:, base + TT:base + TT + 4]),
                            reads=[b_xs], writes=[b_xs])
                    else:
                        f = o - 8
                        p.op("scalar", lambda e, f=f, pb=pb: e.activation(
                            sg_t, PS[pb], AF.Silu, bias=in_b_pp[:, 8 + f:8 + f + 1]),
                            reads=[pB[pb], b_inb], writes=[b_sg])
                        wb = wv_t[f % 2]
                        bw = b_wv[f % 2]
                        p.op("vector", lambda e, f=f, wb=wb: e.tensor_mul(
                            wb, u_t[:, f * TT:(f + 1) * TT], sg_t),
                            reads=[b_u, b_sg], writes=[bw])
                        p.op("gpsimd", lambda e, b=b, f=f, tt=tt, wb=wb: e.dma_start(
                            out=w_h[b, f * 128:(f + 1) * 128, tt * TT:(tt + 1) * TT], in_=wb),
                            reads=[bw], writes=[b_wh], inc=16)

                if ln_from_x:
                    p.op("gpsimd", lambda e, b=b, tt=tt, xin=xin: e.dma_start(
                        out=xnb_h[b, :, tt * TT:(tt + 1) * TT].rearrange(
                            "(f p) n -> f p n", p=128).transpose([1, 0, 2]),
                        in_=xin.rearrange("p (f n) -> p f n", n=TT)),
                        reads=[bx], writes=[b_srcb], inc=16)
                if DBG_STAGE < 6:
                    continue

                def emit_zy(b, tt, u_t, b_u):
                    p.group("tensor", [
                        (lambda e, k=k, u_t=u_t: e.matmul(
                            PS["Ga"][0:S, 0:TT], ssm_w_n[:, k * S:(k + 1) * S],
                            u_t[:, k * TT:(k + 1) * TT], start=(k == 0), stop=(k == 7)))
                        for k in range(8)], reads=[b_ssmn, b_u], writes=[pB["Ga"]])
                    p.op("vector", lambda e, b=b, tt=tt: e.tensor_scalar(
                        z_sb[:, b * T + tt * TT:b * T + (tt + 1) * TT], PS["Ga"][0:S, 0:TT],
                        Bv_sb, zb_s, ALU.mult, ALU.add), reads=[pB["Ga"], bBv, b_zb], writes=[b_z])
                    p.group("tensor", [
                        (lambda e, k=k, u_t=u_t: e.matmul(
                            PS["D"], W2b[:, k * LC:(k + 1) * LC], u_t[:, k * TT:(k + 1) * TT],
                            start=(k == 0), stop=(k == 7)))
                        for k in range(8)], reads=[b_W2, b_u], writes=[pB["D"]])
                    if DBG_STAGE < 7:
                        return
                    ys = ysta[tt % 2]
                    ysdst = ys.rearrange("p (a n) -> p a n", n=256)[:, :, 128:256]
                    p.op("scalar", lambda e, ysdst=ysdst: e.add(
                        ysdst, PS["D"].rearrange("p (a n) -> p a n", n=128), bk),
                        reads=[pB["D"], b_bk], writes=[b_ysta[tt % 2]])
                    yh_dst = bass.AP(y_hbm.tensor, b * 128 * YPW + tt * 4 * 256,
                                     [[YPW, 128], [1, 1024]])
                    p.op("gpsimd", lambda e, ys=ys, yh_dst=yh_dst: e.dma_start(out=yh_dst, in_=ys),
                        reads=[b_ysta[tt % 2]], writes=[b_yh], inc=16)


                if pending_zy is not None:
                    pending_zy()
                pending_zy = (lambda b=b, tt=tt, u_t=u_t, b_u=b_u:
                              emit_zy(b, tt, u_t, b_u))


        if pending_zy is not None:
            pending_zy()
            pending_zy = None

        if DBG_PASSB:
            p.group("tensor", [
                (lambda e, sig=sig: e.matmul(
                    PS["Gb"][0:S, 0:NSEQ * NCH],
                    PTb16[:, (LC - 1 - sig) * S:(LC - sig) * S],
                    z_sb.rearrange("s (b c q) -> s b c q", c=NCH, q=LC)[:, :, :, sig],
                    start=(sig == 0), stop=(sig == LC - 1)))
                for sig in range(LC)], reads=[b_PTb, b_z], writes=[pB["Gb"]])
            p.op("vector", lambda e: e.tensor_copy(q_sb, PS["Gb"][0:S, 0:NSEQ * NCH]),
                 reads=[pB["Gb"]], writes=[b_q])
            p.op("vector", lambda e: e.memset(hall[:, 0:NSEQ], 0.0), reads=[], writes=[b_h])
            for i in range(NCH):
                p.op("tensor", lambda e, i=i: e.matmul(
                    PS["Gb"][0:S, 300:300 + NSEQ], PTw[:, LC * S:(LC + 1) * S],
                    hall[:, i * NSEQ:(i + 1) * NSEQ],
                    start=True, stop=True), reads=[bPT, b_h], writes=[pB["Gb"]])
                p.op("vector", lambda e, i=i: e.tensor_add(
                    hall[:, (i + 1) * NSEQ:(i + 2) * NSEQ], PS["Gb"][0:S, 300:300 + NSEQ],
                    q_sb[:, i::NCH]),
                    reads=[pB["Gb"], b_q], writes=[b_h])
            hallT = hall.rearrange("s (i b) -> s i b", b=NSEQ)[:, 0:NCH, :].transpose([0, 2, 1])
            p.op("vector", lambda e, hallT=hallT: e.tensor_copy(hallT_sb, hallT),
                 reads=[b_h], writes=[b_hT])
            p.op("tensor", lambda e: e.matmul(
                PS["Gb"][0:NSEQ * NCH, 320:320 + LC], hallT_sb, Vmat[:, 1:LC + 1],
                start=True, stop=True), reads=[b_hT, bVm], writes=[pB["Gb"]])
            p.op("vector", lambda e: e.tensor_copy(ccr_b, PS["Gb"][0:NSEQ * NCH, 320:320 + LC]),
                 reads=[pB["Gb"]], writes=[b_ccr])

        for b in range(NSEQ):
            if not DBG_PASSB:
                break
            p.op("gpsimd", lambda e, b=b: e.dma_start(out=cc_fl, in_=ccr_b[b * NCH:(b + 1) * NCH, :]),
                 reads=[b_ccr], writes=[b_ccfl], inc=16)
            yr = b % 2
            ysk_src = bass.AP(y_hbm.tensor, b * 128 * YPW, [[YPW - 1, 128], [1, YPW]])
            p.op("sync", lambda e, ysk_src=ysk_src, yr=yr: e.dma_start(
                out=ysk[yr], in_=ysk_src),
                reads=[b_yh], writes=[b_ysk[yr]], inc=16)
            ydata = ysk[yr].rearrange("p (c q) -> p c q", q=256)[:, :, 128:256]
            for nt in range(4):
                p.op("tensor", lambda e, nt=nt, ydata=ydata: e.matmul(
                    PS["D"][0:1, 0:512],
                    onesc_b, ydata[:, nt * 4:(nt + 1) * 4, :],
                    start=True, stop=True),
                    reads=[b_onesb, b_ysk[yr]], writes=[pB["D"]])
                p.op("vector", lambda e, nt=nt: e.tensor_add(
                    c_fl[:, nt * 512:(nt + 1) * 512], PS["D"][0:1, 0:512],
                    cc_fl[:, nt * 512:(nt + 1) * 512]),
                    reads=[pB["D"], b_ccfl], writes=[b_cfl])
            p.op("gpsimd", lambda e, b=b: e.dma_start(out=c_h[b:b + 1, :], in_=c_fl),
                 reads=[b_cfl], writes=[b_ch], inc=16)

        for b in range(NSEQ):
            if not DBG_PASSB:
                break
            p.op("gpsimd", lambda e, b=b: e.dma_start(
                out=cB, in_=c_h[b:b + 1, :].partition_broadcast(128)),
                reads=[b_ch], writes=[b_cB], inc=16)

            def emit_y(b, tt):
                yy = y_t[tt % 2]
                by = b_y[tt % 2]
                for f in range(8):
                    wl = wld_t[f % 2]
                    bw = b_wld[f % 2]
                    p.op("sync", lambda e, b=b, f=f, tt=tt, wl=wl: e.dma_start(
                        out=wl, in_=w_h[b, f * 128:(f + 1) * 128, tt * TT:(tt + 1) * TT]),
                        reads=[b_wh], writes=[bw], inc=16)
                    p.op("vector", lambda e, f=f, tt=tt, wl=wl, yy=yy: e.scalar_tensor_tensor(
                        yy[:, f * TT:(f + 1) * TT], cB[:, tt * TT:(tt + 1) * TT],
                        D_pp[:, f:f + 1], wl, ALU.add, ALU.mult),
                        reads=[bw, b_cB, b_D], writes=[by])

            def emit_op(b, tt):
                yy = y_t[tt % 2]
                by = b_y[tt % 2]
                for o in range(8):
                    ri2 = o % 2
                    pb = ("A", "B")[ri2]
                    rl = res_t[ri2]
                    brl = b_res[ri2]
                    resb_fm = xnb_h if ln_from_x else srcb_fm
                    resb_hb = b_srcb if ln_from_x else b_srcb
                    p.op("sync", lambda e, b=b, o=o, tt=tt, rl=rl, resb_fm=resb_fm: e.dma_start(
                        out=rl, in_=resb_fm[b, o * 128:(o + 1) * 128, tt * TT:(tt + 1) * TT]),
                        reads=[resb_hb], writes=[brl], inc=16)
                    p.group("tensor", [
                        (lambda e, o=o, k=k, pb=pb, yy=yy: e.matmul(
                            PS[pb], out_w_b[:, k * 1024 + o * 128:k * 1024 + (o + 1) * 128],
                            yy[:, k * TT:(k + 1) * TT], start=(k == 0), stop=(k == 7)))
                        for k in range(8)], reads=[b_outw, by], writes=[pB[pb]])
                    o16 = ost16_t[ri2]
                    p.op("vector", lambda e, o=o, pb=pb, rl=rl, o16=o16: e.scalar_tensor_tensor(
                        o16, PS[pb], out_b_pp[:, o:o + 1], rl, ALU.add, ALU.add),
                        reads=[pB[pb], b_ob, brl], writes=[b_o16[ri2]])
                    p.op("gpsimd", lambda e, b=b, o=o, tt=tt, o16=o16: e.dma_start(
                        out=dstb_fm[b, o * 128:(o + 1) * 128, tt * TT:(tt + 1) * TT], in_=o16),
                        reads=[b_o16[ri2]], writes=[b_dstb], inc=16)

            emit_y(b, 0)
            for tt in range(NTT):
                if tt + 1 < NTT:
                    emit_y(b, tt + 1)
                emit_op(b, tt)
            if wconv_hbuf is not None:
                emit_wconv(30)
        sb.reset()

    def conv_phase(srcb_hbuf, dstb_hbuf, w16_hbuf):
        sb.mark()
        c1T = sb.alloc(128, 8 * 1536, BF16)
        c4T = sb.alloc(128, 8 * 512, BF16)
        c5T = sb.alloc(128, 8 * 3072, BF16)
        c1b_s = sb.alloc(128, 4, F32)
        c1b2 = sb.alloc(128, 4, F32)
        xm2b_r = [sb.alloc(128, 8 * (T + 2), BF16) for _ in range(2)]
        ocb = sb.alloc(128, 8 * (T + 2), BF16)
        res_t = [sb.alloc(128, TT, BF16) for _ in range(2)]
        tmp_t = [sb.alloc(128, TT, F32) for _ in range(2)]
        ost16_t = [sb.alloc(128, TT, BF16) for _ in range(2)]
        b_o16 = [Buf(), Buf()]
        (b_c1, b_c4, b_c5, b_cb1, b_cb2, b_oc) = (Buf() for _ in range(6))
        b_xm2_r = [Buf(), Buf()]
        b_res = [Buf(), Buf()]
        b_tmp = [Buf(), Buf()]

        p.group("sync", [
            (lambda e, f=f: e.dma_start(out=c1T[:, f * 1536:(f + 1) * 1536],
                                        in_=c1w16[f * 128:(f + 1) * 128, :])) for f in range(8)],
            reads=[w16_hbuf], writes=[b_c1], inc_last=16 * 8)
        p.group("sync", [
            (lambda e, f=f: e.dma_start(out=c4T[:, f * 512:(f + 1) * 512],
                                        in_=c4w16[f * 128:(f + 1) * 128, :])) for f in range(8)],
            reads=[w16_hbuf], writes=[b_c4], inc_last=16 * 8)
        p.group("sync", [
            (lambda e, f=f: e.dma_start(out=c5T[:, f * 3072:(f + 1) * 3072],
                                        in_=c5w16[f * 128:(f + 1) * 128, :])) for f in range(8)],
            reads=[w16_hbuf], writes=[b_c5], inc_last=16 * 8)
        p.op("sync", lambda e: e.dma_start(out=c1b_s, in_=c1b), writes=[b_cb1], inc=16)
        p.op("vector", lambda e: e.tensor_scalar(c1b2, c1b_s, float(BN), None, ALU.mult),
             reads=[b_cb1], writes=[b_cb2])
        p.group("vector",
                [(lambda e, f=f, r=r: e.memset(xm2b_r[r][:, f * (T + 2):f * (T + 2) + 1], 0.0))
                 for r in range(2) for f in range(8)] +
                [(lambda e, f=f, r=r: e.memset(
                    xm2b_r[r][:, f * (T + 2) + 1 + T:(f + 1) * (T + 2)], 0.0))
                 for r in range(2) for f in range(8)] +
                [(lambda e, f=f: e.memset(ocb[:, f * (T + 2):f * (T + 2) + 1], 0.0))
                 for f in range(8)] +
                [(lambda e, f=f: e.memset(ocb[:, f * (T + 2) + 1 + T:(f + 1) * (T + 2)], 0.0))
                 for f in range(8)],
                writes=[b_xm2_r[0], b_xm2_r[1], b_oc])

        for b in range(NSEQ):
            xm2b = xm2b_r[b % 2]
            b_xm2 = b_xm2_r[b % 2]
            p.group("sync", [
                (lambda e, b=b, f=f, xm2b=xm2b: e.dma_start(
                    out=xm2b[:, f * (T + 2) + 1:f * (T + 2) + 1 + T],
                    in_=xm2b_h[b, f * 128:(f + 1) * 128, :])) for f in range(8)],
                reads=[srcb_hbuf], writes=[b_xm2], inc_last=16 * 8)
            for tt in range(NTT):
                for c in range(4):
                    pb = ("A", "B")[c % 2]
                    fns = []
                    n = 0
                    for kap in range(3):
                        for f in range(8):
                            fns.append(lambda e, c=c, kap=kap, f=f, tt=tt, n=n, pb=pb,
                                        xm2b=xm2b: e.matmul(
                                PS[pb], c1T[:, f * 1536 + kap * 512 + c * 128:
                                            f * 1536 + kap * 512 + (c + 1) * 128],
                                xm2b[:, f * (T + 2) + tt * TT + kap:
                                     f * (T + 2) + tt * TT + kap + TT],
                                start=(n == 0), stop=(n == 23)))
                            n += 1
                    p.group("tensor", fns, reads=[b_c1, b_xm2], writes=[pB[pb]])
                    p.op("scalar", lambda e, c=c, tt=tt, pb=pb: e.activation(
                        ocb[:, c * (T + 2) + 1 + tt * TT:c * (T + 2) + 1 + (tt + 1) * TT],
                        PS[pb], AF.Relu, bias=c1b2[:, c:c + 1], scale=cst[:, 1:2]),
                        reads=[pB[pb], b_cb2, b_cst], writes=[b_oc])
                for c in range(4):
                    pb = ("A", "B")[c % 2]
                    p.group("tensor", [
                        (lambda e, c=c, f=f, tt=tt, pb=pb, xm2b=xm2b: e.matmul(
                            PS[pb], c4T[:, f * 512 + c * 128:f * 512 + (c + 1) * 128],
                            xm2b[:, f * (T + 2) + tt * TT + 1:f * (T + 2) + tt * TT + 1 + TT],
                            start=(f == 0), stop=(f == 7))) for f in range(8)],
                        reads=[b_c4, b_xm2], writes=[pB[pb]])
                    p.op("scalar", lambda e, c=c, tt=tt, pb=pb: e.activation(
                        ocb[:, (4 + c) * (T + 2) + 1 + tt * TT:
                            (4 + c) * (T + 2) + 1 + (tt + 1) * TT],
                        PS[pb], AF.Relu), reads=[pB[pb]], writes=[b_oc])
            for tt in range(NTT):
                for o in range(8):
                    ri2 = o % 2
                    pb = ("A", "B")[ri2]
                    rl = res_t[ri2]
                    p.op("sync", lambda e, b=b, o=o, tt=tt, rl=rl: e.dma_start(
                        out=rl, in_=xm2b_h[b, o * 128:(o + 1) * 128, tt * TT:(tt + 1) * TT]),
                        reads=[srcb_hbuf], writes=[b_res[ri2]], inc=16)
                    fns = []
                    n = 0
                    for kap in range(3):
                        for f in range(8):
                            fns.append(lambda e, o=o, kap=kap, f=f, tt=tt, n=n, pb=pb: e.matmul(
                                PS[pb], c5T[:, f * 3072 + kap * 1024 + o * 128:
                                            f * 3072 + kap * 1024 + (o + 1) * 128],
                                ocb[:, f * (T + 2) + tt * TT + kap:
                                    f * (T + 2) + tt * TT + kap + TT],
                                start=(n == 0), stop=(n == 23)))
                            n += 1
                    p.group("tensor", fns, reads=[b_c5, b_oc], writes=[pB[pb]])
                    tm = tmp_t[ri2]
                    p.op("scalar", lambda e, pb=pb, tm=tm: e.activation(
                        tm, PS[pb], AF.Relu, scale=cst[:, 1:2]),
                        reads=[pB[pb], b_cst], writes=[b_tmp[ri2]])
                    o16 = ost16_t[ri2]
                    p.op("vector", lambda e, tm=tm, rl=rl, o16=o16: e.tensor_add(o16, tm, rl),
                         reads=[b_tmp[ri2], b_res[ri2]], writes=[b_o16[ri2]])
                    p.op("gpsimd", lambda e, b=b, o=o, tt=tt, o16=o16: e.dma_start(
                        out=s16_h[b, o * 128:(o + 1) * 128, tt * TT:(tt + 1) * TT], in_=o16),
                        reads=[b_o16[ri2]], writes=[dstb_hbuf], inc=16)
        sb.reset()

    def ffn_phase(xn_hbuf, s_hbuf, w16_hbuf):
        sb.mark()
        wgb = sb.alloc(128, 8 * H3, BF16)
        wub = sb.alloc(128, 8 * H3, BF16)
        wdb = sb.alloc(128, 24 * D, BF16)
        hh_all = sb.alloc(128, 24 * TT, BF16)
        b_hh = Buf()
        xn_t = sb.alloc(128, 8 * TT, BF16)
        b_xnt = Buf()
        g_t = sb.alloc(128, TT, F32)
        b_g = Buf()
        sgf = sb.alloc(128, 8 * TT, BF16)
        b_sgf = Buf()
        s_t = [sb.alloc(128, TT, BF16) for _ in range(2)]
        b_st = [Buf(), Buf()]
        x_tm2 = [sb.alloc(128, D, F32) for _ in range(2)]
        b_xt = [Buf(), Buf()]
        o_tm = [sb.alloc(128, D, F32) for _ in range(2)]
        b_ot = [Buf(), Buf()]
        b_wg, b_wu, b_wd = Buf(), Buf(), Buf()
        b_out = Buf(nowaw=True)

        for wsrc, wdst, bw, nt, n in ((wg16, wgb, b_wg, 8, H3), (wu16, wub, b_wu, 8, H3),
                                      (wd16, wdb, b_wd, 24, D)):
            p.group("sync", [
                (lambda e, wsrc=wsrc, wdst=wdst, f=f, n=n: e.dma_start(
                    out=wdst[:, f * n:(f + 1) * n], in_=wsrc[f * 128:(f + 1) * 128, :]))
                for f in range(nt)], reads=[w16_hbuf], writes=[bw], inc_last=16 * nt)

        for b in range(NSEQ):
            for tt in range(NTT):
                p.group("sync", [
                    (lambda e, b=b, f=f, tt=tt: e.dma_start(
                        out=xn_t[:, f * TT:(f + 1) * TT],
                        in_=xnb_h[b, f * 128:(f + 1) * 128, tt * TT:(tt + 1) * TT]))
                    for f in range(8)], reads=[xn_hbuf], writes=[b_xnt], inc_last=16 * 8)
                for hc in range(24):
                    p.group("tensor", [
                        (lambda e, hc=hc, k=k: e.matmul(
                            PS["A"], wgb[:, k * H3 + hc * 128:k * H3 + (hc + 1) * 128],
                            xn_t[:, k * TT:(k + 1) * TT], start=(k == 0), stop=(k == 7)))
                        for k in range(8)], reads=[b_wg, b_xnt], writes=[pB["A"]])
                    p.group("tensor", [
                        (lambda e, hc=hc, k=k: e.matmul(
                            PS["B"], wub[:, k * H3 + hc * 128:k * H3 + (hc + 1) * 128],
                            xn_t[:, k * TT:(k + 1) * TT], start=(k == 0), stop=(k == 7)))
                        for k in range(8)], reads=[b_wu, b_xnt], writes=[pB["B"]])
                    p.op("scalar", lambda e: e.activation(g_t, PS["A"], AF.Silu),
                         reads=[pB["A"]], writes=[b_g])
                    p.op("vector", lambda e, hc=hc: e.tensor_mul(
                        hh_all[:, hc * TT:(hc + 1) * TT], g_t, PS["B"]),
                        reads=[b_g, pB["B"]], writes=[b_hh])
                for dc in range(8):
                    pb = ("C", "D")[dc % 2]
                    p.group("tensor", [
                        (lambda e, dc=dc, hc=hc, pb=pb: e.matmul(
                            PS[pb], wdb[:, hc * D + dc * 128:hc * D + (dc + 1) * 128],
                            hh_all[:, hc * TT:(hc + 1) * TT], start=(hc == 0), stop=(hc == 23)))
                        for hc in range(24)], reads=[b_wd, b_hh], writes=[pB[pb]])
                    p.op("scalar", lambda e, pb=pb: e.activation(g_t, PS[pb], AF.Silu),
                         reads=[pB[pb]], writes=[b_g])
                    sl = s_t[dc % 2]
                    p.op("sync", lambda e, b=b, dc=dc, tt=tt, sl=sl: e.dma_start(
                        out=sl, in_=s16_h[b, dc * 128:(dc + 1) * 128, tt * TT:(tt + 1) * TT]),
                        reads=[s_hbuf], writes=[b_st[dc % 2]], inc=16)
                    p.op("vector", lambda e, dc=dc, sl=sl: e.tensor_mul(
                        sgf[:, dc * TT:(dc + 1) * TT], g_t, sl),
                        reads=[b_g, b_st[dc % 2]], writes=[b_sgf])
                for blk in range(TT // 128):
                    half = blk % 2
                    acc = ("G", "H")[half]
                    xr = x_tm2[half]
                    p.op("sync", lambda e, b=b, tt=tt, blk=blk, xr=xr: e.dma_start(
                        out=xr, in_=x_in[b, tt * TT + blk * 128:tt * TT + (blk + 1) * 128, :]),
                        writes=[b_xt[half]], inc=16)
                    psb = PS[acc].bitcast(BF16)
                    p.group("tensor", [
                        (lambda e, dc=dc, blk=blk, psb=psb: e.transpose(
                            psb[:, dc * 128:(dc + 1) * 128],
                            sgf[:, dc * TT + blk * 128:dc * TT + (blk + 1) * 128], ident_b))
                        for dc in range(8)], reads=[b_sgf, b_identb], writes=[pB[acc]])
                    ob = o_tm[half]
                    p.op("vector", lambda e, ob=ob, xr=xr, psb=psb: e.tensor_add(
                        ob, psb[:, 0:D], xr),
                        reads=[pB[acc], b_xt[half]], writes=[b_ot[half]])
                    p.op("gpsimd", lambda e, b=b, tt=tt, blk=blk, ob=ob: e.dma_start(
                        out=out_t[b, tt * TT + blk * 128:tt * TT + (blk + 1) * 128, :], in_=ob),
                        reads=[b_ot[half]], writes=[b_out], inc=16)
        sb.reset()

    hbb = {"xn": Buf(nowaw=True), "xm1": Buf(nowaw=True), "xm2": Buf(nowaw=True),
           "s": Buf(nowaw=True), "w16": Buf(nowaw=True)}
    if "p1" in phases:
        mamba_phase("m1", ln_from_x=True,
                    srcb_hbuf=hbb["xn"], dstb_fm=xm1b_h, dstb_hbuf=hbb["xm1"])
        p.fence()
    if "p2" in phases:
        mamba_phase("m2", ln_from_x=False,
                    srcb_fm=xm1b_h, srcb_hbuf=hbb["xm1"],
                    dstb_fm=xm2b_h, dstb_hbuf=hbb["xm2"],
                    wconv_hbuf=(hbb["w16"] if ("p3" in phases or "p4" in phases) else None))
        p.fence()
    sb.reset()
    if "p3" in phases:
        conv_phase(srcb_hbuf=hbb["xm2"], dstb_hbuf=hbb["s"], w16_hbuf=hbb["w16"])
        p.fence()
    if "p4" in phases:
        ffn_phase(xn_hbuf=hbb["xn"], s_hbuf=hbb["s"], w16_hbuf=hbb["w16"])
    p.raw("sync", lambda e: None,
          waits=[(k, v) for k, v in p.cnt.items() if k != "sync"], inc=0)
    p.emit()
    return nc


def _prep_inputs(kw):
    def pp(v, nchunk):
        return np.ascontiguousarray(np.asarray(v, np.float32).reshape(nchunk, 128).T)
    d = {}
    for m in ("m1", "m2"):
        P = f"L0_attn_{m}_"
        d[f"{m}_in_w"] = np.ascontiguousarray(kw[P + "in_w"], np.float32)
        d[f"{m}_in_b"] = pp(kw[P + "in_b"], 16)
        cw = np.asarray(kw[P + "conv_w"], np.float32).reshape(8, 128, 4)
        d[f"{m}_conv_w"] = np.ascontiguousarray(np.transpose(cw, (1, 0, 2)).reshape(128, 32))
        d[f"{m}_conv_b"] = pp(kw[P + "conv_b"], 8)
        d[f"{m}_ssm_w"] = np.ascontiguousarray(kw[P + "ssm_w"], np.float32)
        d[f"{m}_ssm_wT"] = np.ascontiguousarray(np.asarray(kw[P + "ssm_w"], np.float32).T)
        d[f"{m}_ssm_b"] = np.ascontiguousarray(np.asarray(kw[P + "ssm_b"], np.float32).reshape(S, 1))
        d[f"{m}_A"] = np.ascontiguousarray(kw[P + "A"], np.float32)
        d[f"{m}_B"] = np.ascontiguousarray(np.asarray(kw[P + "B"], np.float32).reshape(S, 1))
        d[f"{m}_C"] = np.ascontiguousarray(np.asarray(kw[P + "C"], np.float32).reshape(S, 1))
        d[f"{m}_D"] = pp(kw[P + "D"], 8)
        d[f"{m}_out_w"] = np.ascontiguousarray(kw[P + "out_w"], np.float32)
        d[f"{m}_out_b"] = pp(kw[P + "out_b"], 8)
    d["c1wT"] = np.ascontiguousarray(
        np.transpose(np.asarray(kw["L0_attn_c1_w"], np.float32), (1, 2, 0)).reshape(D, 3 * 512))
    d["c1b"] = pp(kw["L0_attn_c1_b"], 4)
    d["c4wT"] = np.ascontiguousarray(np.asarray(kw["L0_attn_c4_w"], np.float32)[:, :, 0].T)
    d["c5wT"] = np.ascontiguousarray(
        np.transpose(np.asarray(kw["L0_attn_c5_w"], np.float32), (1, 2, 0)).reshape(D, 3 * D))
    d["wgT"] = np.ascontiguousarray(np.asarray(kw["L0_ffn_w_gate"], np.int32).T)
    d["wuT"] = np.ascontiguousarray(np.asarray(kw["L0_ffn_w_up"], np.int32).T)
    d["wdT"] = np.ascontiguousarray(np.asarray(kw["L0_ffn_w_down"], np.int32).T)
    return d


def kernel(**kw):
    ncores = 8
    x = np.ascontiguousarray(np.asarray(kw["L0_attn_x"], dtype=np.float32))
    shared = _prep_inputs(kw)
    nc = build_program()
    in_maps = []
    for c in range(ncores):
        m = dict(shared)
        m["x"] = np.ascontiguousarray(x[c * NSEQ:(c + 1) * NSEQ])
        in_maps.append(m)
    res = run_bass_kernel_spmd(nc, in_maps, core_ids=list(range(ncores)))
    return np.concatenate([r["out"] for r in res.results], axis=0)
